# version 21
# baseline (speedup 1.0000x reference)
import numpy as np
from contextlib import ExitStack

import concourse.bass as bass
import concourse.mybir as mybir
from concourse.bass_utils import run_bass_kernel_spmd

F32 = mybir.dt.float32
F32R = mybir.dt.float32r
BF16 = mybir.dt.bfloat16
AF = mybir.ActivationFunctionType
ALU = mybir.AluOpType

NCORES = 8
D = 1024
NH = 16
HD = 64
E = 8
II = 2048
NF = D // 128
EPS = 1e-5


class Prog:
    ENGS = ("sync", "gpsimd", "tensor", "vector", "scalar")

    def __init__(self):
        self.thunks = {e: [] for e in self.ENGS}
        self.sems = {}
        self.sem_handles = {}
        self.waited = {e: {} for e in self.ENGS}

    def declare(self, *names):
        for n in names:
            self.sems[n] = 0

    def wait(self, eng, sem, val):
        assert val <= self.sems[sem], f"wait {sem}>={val} > emitted {self.sems[sem]}"
        if val <= self.waited[eng].get(sem, 0):
            return
        self.waited[eng][sem] = val
        self.thunks[eng].append(lambda e, s=sem, v=val: e.wait_ge(self.sem_handles[s], v))

    def op(self, eng, build, inc=None, amt=1):
        if inc is None:
            self.thunks[eng].append(build)
            return None
        self.sems[inc] += amt
        self.thunks[eng].append(
            lambda e, b=build, s=inc: b(e).then_inc(self.sem_handles[s], amt))
        return self.sems[inc]

    def dma(self, eng, out, in_, inc, slow=False):
        if slow:
            return self.op(eng, lambda e: e.dma_start(
                out=out, in_=in_, allow_slow_non_contiguous=True), inc, 16)
        return self.op(eng, lambda e: e.dma_start(out=out, in_=in_), inc, 16)


def build_program(T=2048, TS=512):
    NT = T // TS
    TC = T // 128
    NH2 = max(1, NT // 2)
    W1 = min(1024, T)
    nc = bass.Bass("TRN2", target_bir_lowering=False)
    p = Prog()
    p.declare("dp", "dxx", "dxg", *[f"dx{i}" for i in range(8)], "dst", "swk", "swo", "sbo", "sci", "scg", "swb", "sr",
              "pe", "dve", "act", "gpc", "cc",
              *[f"r{i}" for i in range(6)], *[f"m{i}" for i in range(4)],
              "q0", "q1", "e0", "e1", "e2", "e3", "e4", "smod", "sg2")

    dx = nc.dram_tensor("xT", [D, T], F32, kind="ExternalInput").ap()
    dWq = nc.dram_tensor("Wq", [128, D], F32, kind="ExternalInput").ap()
    dWk = nc.dram_tensor("Wk", [D, D], F32, kind="ExternalInput").ap()
    dWv = nc.dram_tensor("Wv", [128, D], F32, kind="ExternalInput").ap()
    dWbeta = nc.dram_tensor("Wbeta", [D, NH], F32, kind="ExternalInput").ap()
    db_beta = nc.dram_tensor("b_beta", [NH], F32, kind="ExternalInput").ap()
    dWo = nc.dram_tensor("Wo", [D, D], F32, kind="ExternalInput").ap()
    db_o = nc.dram_tensor("b_o", [D], F32, kind="ExternalInput").ap()
    dln_g = nc.dram_tensor("ln_g", [D], F32, kind="ExternalInput").ap()
    dln_b = nc.dram_tensor("ln_b", [D], F32, kind="ExternalInput").ap()
    dWg = nc.dram_tensor("Wg", [D, E], F32, kind="ExternalInput").ap()
    dW11 = nc.dram_tensor("W11", [E, D, II], F32, kind="ExternalInput").ap()
    dW12 = nc.dram_tensor("W12", [E, D, II], F32, kind="ExternalInput").ap()
    dW2 = nc.dram_tensor("W2", [E, II, D], F32, kind="ExternalInput").ap()
    dc = nc.dram_tensor("cvec", [1, D], F32, kind="ExternalInput").ap()
    dwmod = nc.dram_tensor("wmod", [D, (6 * D) // NCORES], F32, kind="ExternalInput").ap()
    dpmask = nc.dram_tensor("pmask", [128, NCORES * NF], BF16, kind="ExternalInput").ap()
    dy = nc.dram_tensor("yT", [D, T], F32, kind="ExternalOutput").ap()
    dcc_in = nc.dram_tensor("cc_in", [128, 2 * NF], F32)
    dcc_out = nc.dram_tensor("cc_out", [NCORES * 128, 2 * NF], F32)

    es = ExitStack()
    sbuf = lambda n, sh, dt: es.enter_context(nc.sbuf_tensor(n, sh, dt))

    Ax = sbuf("Ax", [128, NF, T], F32)
    Ascr = sbuf("Ascr", [128, 4 * T], F32)
    Abf = sbuf("Abf", [128, NF, T], BF16)
    Ay = sbuf("Ay", [128, 4 * T], F32)
    Aw = sbuf("Aw", [128, 6 * 2048], BF16)
    aux = sbuf("aux", [128, T], F32)
    comp = aux[0:48, :]
    rows = aux[:, :]
    sq_bf_t = sbuf("sqbf", [128, T], BF16)
    mod_cols = sbuf("mod_cols", [128, 10 * NF], F32)
    smalls = sbuf("smalls", [128, 448], F32)
    smallsb = sbuf("smallsb", [128, 32 + TS + D + 2048], BF16)
    wcomb = sbuf("wcomb", [128, NF * 80], BF16)
    wg_sb = sbuf("wg_sb", [128, NF * E], F32)
    gathsb = sbuf("gathsb", [128, NCORES, 2 * NF], F32)
    summ = sbuf("summ", [128, 2 * NF], F32)
    minit = sbuf("minit", [128, 2 * NF], F32)
    pmask_sb = sbuf("pmask_sb", [128, NCORES * NF], BF16)

    Ascr_bf = Ascr[:].bitcast(BF16)
    gtm = Ay[:, 0:TC * E].rearrange("p (a b) -> p a b", b=E)
    wtm = Ay[:, TC * E:2 * TC * E].rearrange("p (a b) -> p a b", b=E)
    st2 = Ay[:, 2 * TC * E:3 * TC * E].rearrange("p (a b) -> p a b", b=E)
    st1 = Ay[:, 3 * TC * E:3 * TC * E + 5 * TC].rearrange("p (a b) -> p a b", b=5)
    Ay_bf = Ay[:].bitcast(BF16)
    Aw_f32 = Aw[:].bitcast(F32)

    pb = [es.enter_context(nc.psum_tensor(f"pb{i}", [128, 1024], F32)) for i in range(4)]

    def bank(i):
        return pb[i // 2][:, (i % 2) * 512:(i % 2) * 512 + 512]

    pr_rows = Aw[0:2, 8192:8192 + T]
    bank_free = {}

    def pe_claim(banks):
        for b in banks:
            st = bank_free.pop(b, None)
            if st is not None:
                if isinstance(st, list):
                    for sem, val in st:
                        p.wait("tensor", sem, val)
                else:
                    p.wait("tensor", st[0], st[1])

    def scr(i):
        return Ascr[:, i * T:(i + 1) * T]

    def w_chunks(view, nk):
        return view.rearrange("p (k c) -> p k c", k=nk)

    c_col = smalls[:, 0:NF]
    p.dma("sync", c_col, dc.rearrange("1 (a b) -> b a", b=128), "dp", slow=True)
    lng_col = smalls[:, 16:16 + NF]
    lnb_col = smalls[:, 32:32 + NF]
    p.dma("sync", lng_col, dln_g.rearrange("(a b) -> b a", b=128), "dp", slow=True)
    p.dma("sync", lnb_col, dln_b.rearrange("(a b) -> b a", b=128), "dp", slow=True)
    bb16 = smalls[64:80, 48:49]
    p.dma("sync", bb16, db_beta.unsqueeze(1), "dp")
    v_wcz = p.op("vector", lambda e: e.memset(wcomb[:], 0.0), "dve")
    p.wait("gpsimd", "dve", v_wcz)
    p.dma("gpsimd", wcomb[:].rearrange("p (f h) -> p f h", h=80)[:, :, 64:80],
          dWbeta.rearrange("(f p) h -> p f h", p=128), "swb")
    p.dma("sync", wg_sb[:].rearrange("p (f e) -> p f e", e=E),
          dWg.rearrange("(f p) e -> p f e", p=128), "dp")
    p.dma("sync", pmask_sb[:], dpmask, "dp")
    v_params = p.sems["dp"]

    identr = smalls[:, 64:192]
    v_iota = p.op("gpsimd", lambda e: e.iota(identr, pattern=[[-1, 128]], base=0,
                                             channel_multiplier=1,
                                             allow_small_or_imprecise_dtypes=True), "gpc")
    ident = smalls[:, 192:320]
    p.wait("vector", "gpc", v_iota)
    p.op("vector", lambda e: e.tensor_single_scalar(ident, identr, 0.0, ALU.is_equal), "dve")
    v_id = p.sems["dve"]
    ones_col_bf = smallsb[:, 0:1]
    p.op("vector", lambda e: e.memset(ones_col_bf, 1.0), "dve")
    ones_row_bf = smallsb[0:1, 32:32 + TS]
    p.op("vector", lambda e: e.memset(ones_row_bf, 1.0), "dve")
    v_ones = p.sems["dve"]
    bo_row_bf = smallsb[0:1, 32 + TS:32 + TS + D]
    sel_bf = smallsb[0:8, 32 + TS + D:32 + TS + D + 1024]
    v_selr = p.op("gpsimd", lambda e: e.iota(sel_bf, pattern=[[1, 8], [0, 128]], base=0,
                                             channel_multiplier=-1,
                                             allow_small_or_imprecise_dtypes=True), "gpc")
    p.wait("vector", "gpc", v_selr)
    p.op("vector", lambda e: e.tensor_single_scalar(sel_bf, sel_bf, 0.0, ALU.is_equal), "dve")
    v_sel = p.sems["dve"]
    E3 = smallsb[:, 32 + TS + D + 1024:32 + TS + D + 2048]
    for b0 in (0, 32, 64):
        p.op("gpsimd", lambda e, bb=b0: e.iota(E3[bb:bb + 16, :], pattern=[[1, 16], [0, 64]],
                                               base=0, channel_multiplier=-1,
                                               allow_small_or_imprecise_dtypes=True), "gpc")
    v_e3r = p.sems["gpc"]
    p.wait("vector", "gpc", v_e3r)
    for b0 in (0, 32, 64):
        p.op("vector", lambda e, bb=b0: e.tensor_single_scalar(
            E3[bb:bb + 16, :], E3[bb:bb + 16, :], 0.0, ALU.is_equal), "dve")
    v_e3 = p.sems["dve"]
    v_sel2 = v_sel

    eps_col = smalls[:, 424:425]
    p.op("vector", lambda e: e.memset(eps_col, EPS), "dve")
    v_eps = p.sems["dve"]
    silu_c = smalls[:, 320:320 + NF]
    p.wait("scalar", "dp", v_params)
    v_siluc = p.op("scalar", lambda e: e.activation(silu_c, c_col, AF.Silu), "act")
    v_bob = p.dma("gpsimd", bo_row_bf, db_o.unsqueeze(0), "sbo")
    vx = []
    xsem = []
    for f in range(NF):
        sem = f"dx{f}"
        xsem.append(sem)
        vx.append(p.dma("scalar", Ax[:, f, :], dx[f * 128:(f + 1) * 128, :], sem))

    MSH = (6 * D) // NCORES
    dcc2_in = nc.dram_tensor("cc2_in", [MSH + 2 * 128 * NH], F32)
    dcc2_out = nc.dram_tensor("cc2_out", [NCORES, MSH + 2 * 128 * NH], F32)
    wm_view = [Aw_f32[:, i * MSH:(i + 1) * MSH] for i in range(4)]
    wm_state = [None] * 4
    p.wait("tensor", "act", v_siluc)
    pe_claim([4, 5])
    vmm = 0
    for kc in range(8):
        slot = kc % 4
        if wm_state[slot] is not None:
            p.wait("gpsimd", *wm_state[slot])
            wm_state[slot] = None
        vld = p.dma("gpsimd", wm_view[slot], dwmod[kc * 128:(kc + 1) * 128, :], f"m{slot}")
        p.wait("tensor", f"m{slot}", vld)
        for j2, (o0, w0) in enumerate(((0, 512), (512, MSH - 512))):
            vmm = p.op("tensor", lambda e, s_=slot, k=kc, o=o0, w=w0, j=j2: e.matmul(
                bank(4 + j)[0:1, 0:w], silu_c[:, k:k + 1],
                wm_view[s_][:, o:o + w], start=(k == 0), stop=(k == 7)), "pe")
        wm_state[slot] = ("pe", p.sems["pe"])
    v_modpe = p.sems["pe"]
    p.wait("vector", "pe", vmm)
    p.wait("vector", "dve", p.sems["dve"])
    p.op("vector", lambda e: e.tensor_copy(Ascr[0:1, 0:512], bank(4)[0:1, 0:512]), "dve")
    vev = p.op("vector", lambda e: e.tensor_copy(
        Ascr[32:33, 0:MSH - 512], bank(5)[0:1, 0:MSH - 512]), "dve")
    bank_free[4] = ("dve", vev)
    bank_free[5] = ("dve", vev)
    p.wait("sync", "dve", vev)
    p.dma("sync", dcc2_in.ap()[0:512].unsqueeze(0), Ascr[0:1, 0:512], "smod")
    p.dma("sync", dcc2_in.ap()[512:MSH].unsqueeze(0), Ascr[32:33, 0:MSH - 512], "smod")

    rq_view = [Aw_f32[:, 4 * MSH + i * D:4 * MSH + (i + 1) * D] for i in range(2)]
    rq_stage = scr(3)[:, 0:64]
    for i, dmat in enumerate((dWv, dWq)):
        vld = p.dma("gpsimd", rq_view[i], dmat, f"q{i}")
        p.wait("vector", f"q{i}", vld)
        p.wait("vector", "dve", p.sems["dve"])
        for h in range(NH):
            p.op("vector", lambda e, si_=i, hh=h: e.reduce_sum(
                rq_stage[:, si_ * NH + hh:si_ * NH + hh + 1],
                rq_view[si_][:, hh * HD:(hh + 1) * HD], axis=mybir.AxisListType.X), "dve")
    vrq = p.sems["dve"]
    p.wait("sync", "dve", vrq)
    p.dma("sync", dcc2_in.ap()[MSH:MSH + 2048].rearrange("(pp h) -> pp h", pp=128),
          rq_stage[:, 0:NH], "smod")
    v_in2 = p.dma("sync", dcc2_in.ap()[MSH + 2048:MSH + 4096].rearrange(
        "(pp h) -> pp h", pp=128), rq_stage[:, NH:2 * NH], "smod")
    p.wait("gpsimd", "smod", v_in2)
    v_cc2 = p.op("gpsimd", lambda e: e.collective_compute(
        "AllGather", ALU.bypass, replica_groups=[list(range(NCORES))],
        ins=[dcc2_in.ap()], outs=[dcc2_out.ap()]), "cc")
    p.wait("sync", "cc", v_cc2)
    MQ = MSH // 128
    for j in range(NCORES):
        v_mcols = p.dma("sync", mod_cols[:, j * MQ:(j + 1) * MQ],
                        dcc2_out.ap()[j:j + 1, 0:MSH].rearrange(
                            "a (q pp) -> (a pp) q", pp=128), "smod", slow=True)
    p.wait("gpsimd", "cc", v_cc2)
    p.dma("gpsimd", wcomb[:].rearrange("p (k s) -> p k s", s=80)[:, :, 0:NH],
          dcc2_out.ap()[:, MSH:MSH + 2048].rearrange("j (pp h) -> pp j h", pp=128), "sg2")
    v_wq1g = p.dma("gpsimd", wcomb[:].rearrange("p (k s) -> p k s", s=80)[:, :, 32:32 + NH],
                   dcc2_out.ap()[:, MSH + 2048:MSH + 4096].rearrange(
                       "j (pp h) -> pp j h", pp=128), "sg2")
    p.wait("vector", "smod", p.sems["smod"])
    p.wait("scalar", "smod", p.sems["smod"])
    p.wait("vector", "dve", p.sems["dve"])
    SA, CA1, GA, SB_, CB1, GB, GAG, GAB, SC1, SC2 = [
        mod_cols[:, i * NF:(i + 1) * NF] for i in range(10)]

    def dve(build):
        p.wait("vector", "dve", p.sems["dve"])
        return p.op("vector", build, "dve")

    def clamp_gate(g):
        dve(lambda e: e.tensor_single_scalar(SC1, g, 0.0, ALU.is_ge))
        dve(lambda e: e.tensor_scalar(SC1, SC1, 2.0, -1.0, ALU.mult, ALU.add))
        dve(lambda e: e.tensor_mul(SC2, g, SC1))
        dve(lambda e: e.tensor_scalar_max(SC2, SC2, 0.25))
        dve(lambda e: e.tensor_mul(g, SC2, SC1))

    clamp_gate(GA)
    clamp_gate(GB)
    dve(lambda e: e.tensor_scalar_add(CA1, CA1, 1.0))
    dve(lambda e: e.tensor_scalar_add(CB1, CB1, 1.0))
    p.wait("vector", "dp", v_params)
    dve(lambda e: e.tensor_mul(GAG, GA, lng_col))
    dve(lambda e: e.tensor_mul(GAB, GA, lnb_col))
    v_mod_done = p.sems["dve"]
    v_wqv1 = vrq

    def ln_stats(src, sq_bf, persist=False, pre=None, vpe_guard=None):
        vg = p.sems["pe"] if vpe_guard is None else vpe_guard
        if vg:
            p.wait("scalar", "pe", vg)
        sq2 = Ascr_bf[:, 6 * T:7 * T]
        pe_claim(list(range(2 * NT)))
        vpe_f = []
        for f in range(NF):
            if pre is not None:
                pre(f)
            p.wait("scalar", "act", p.sems["act"])
            if f >= 2:
                p.wait("scalar", "pe", vpe_f[f - 2])
            sq = sq_bf if f % 2 == 0 else sq2
            vsq = p.op("scalar", lambda e, ff=f, q=sq: e.activation(q, src[ff], AF.Square), "act")
            p.wait("tensor", "act", vsq)
            for t in range(NT):
                p.op("tensor", lambda e, ff=f, tt=t: e.matmul(
                    bank(tt)[0:1, 0:TS], ones_col_bf,
                    src[ff][:, tt * TS:(tt + 1) * TS],
                    start=(ff == 0), stop=(ff == NF - 1)), "pe")
                p.op("tensor", lambda e, ff=f, tt=t, q=sq: e.matmul(
                    bank(NT + tt)[0:1, 0:TS], ones_col_bf,
                    q[:, tt * TS:(tt + 1) * TS],
                    start=(ff == 0), stop=(ff == NF - 1)), "pe")
            vpe_f.append(p.sems["pe"])
        v_st = p.sems["pe"]
        ra = scr(2)[0:1, :]
        rb = scr(3)[0:1, :]
        rc = scr(0)[0:1, :]
        p.wait("scalar", "pe", v_st)
        p.wait("scalar", "act", p.sems["act"])
        p.wait("scalar", "dve", p.sems["dve"])
        for t in range(NT):
            p.op("scalar", lambda e, tt=t: e.activation(
                ra[:, tt * TS:(tt + 1) * TS], bank(tt)[0:1, 0:TS], AF.Copy,
                scale=1.0 / D), "act")
            va = p.op("scalar", lambda e, tt=t: e.activation(
                rb[:, tt * TS:(tt + 1) * TS], bank(NT + tt)[0:1, 0:TS], AF.Copy,
                scale=1.0 / D), "act")
            bank_free[t] = ("act", va)
            bank_free[NT + t] = ("act", va)
        p.wait("scalar", "act", p.sems["act"])
        vmu = p.op("scalar", lambda e: e.activation(rc, ra, AF.Square), "act")
        p.wait("vector", "act", vmu)
        p.wait("vector", "dve", p.sems["dve"])
        vvar = p.op("vector", lambda e: e.tensor_sub(rb, rb, rc), "dve")
        p.wait("scalar", "dve", max(vvar, v_eps))
        p.wait("scalar", "act", p.sems["act"])
        vsq2 = p.op("scalar", lambda e: e.activation(
            rc, rb, AF.Sqrt, bias=eps_col[0:1, 0:1]), "act")
        p.wait("vector", "act", vsq2)
        p.wait("vector", "dve", p.sems["dve"])
        vrec = p.op("vector", lambda e: e.reciprocal(rb, rc), "dve")
        if persist:
            p.wait("gpsimd", "dve", vrec)
            p.wait("gpsimd", "pe", p.sems["pe"])
            p.dma("gpsimd", pr_rows[0:1, :], ra, "sr")
            p.dma("gpsimd", pr_rows[1:2, :], rb, "sr")
        return ra, rb, vrec

    NB2 = max(1, T // 1024)
    NTH2 = max(1, min(1024, T) // TS)
    WH = min(1024, T)

    ones_f32row = smalls[0:1, 64:192]
    p.wait("vector", "dve", v_id)
    p.op("vector", lambda e: e.memset(ones_f32row[:, 0:128], 1.0), "dve")
    v_onesf = p.sems["dve"]

    def expand_rows(mu_row, r_row, vrow_dve):
        pe_claim(list(range(8)))
        p.wait("tensor", "dve", max(vrow_dve, v_onesf))
        vmm = 0
        for t in range(NT):
            vmm = p.op("tensor", lambda e, tt=t: e.matmul(
                pb[tt // 2][:, (tt % 2) * TS:(tt % 2) * TS + TS],
                ones_f32row[:, 0:128], mu_row[:, tt * TS:(tt + 1) * TS],
                start=True, stop=True), "pe")
            vmm = p.op("tensor", lambda e, tt=t: e.matmul(
                pb[2 + tt // 2][:, (tt % 2) * TS:(tt % 2) * TS + TS],
                ones_f32row[:, 0:128], r_row[:, tt * TS:(tt + 1) * TS],
                start=True, stop=True), "pe")
        p.wait("vector", "pe", vmm)
        p.wait("vector", "dve", p.sems["dve"])
        vcp = 0
        for h in range(NB2):
            p.op("vector", lambda e, hh=h: e.tensor_copy(
                scr(0)[:, hh * WH:(hh + 1) * WH], pb[hh][:, 0:WH]), "dve")
            vcp = p.op("vector", lambda e, hh=h: e.tensor_copy(
                scr(1)[:, hh * WH:(hh + 1) * WH], pb[2 + hh][:, 0:WH]), "dve")
        for b in range(8):
            bank_free[b] = ("dve", vcp)
        return vcp

    def expand_rows2(rows2, vdep):
        p.wait("tensor", "sr", p.sems["sr"])
        pe_claim(list(range(8)))
        p.wait("tensor", "dve", max(vdep, v_sel2))
        vmm = 0
        for t in range(NT):
            vmm = p.op("tensor", lambda e, tt=t: e.matmul(
                pb[tt // 2][:, (tt % 2) * TS:(tt % 2) * TS + TS],
                sel_bf[0:2, 0:128], rows2[:, tt * TS:(tt + 1) * TS],
                start=True, stop=True), "pe")
            vmm = p.op("tensor", lambda e, tt=t: e.matmul(
                pb[2 + tt // 2][:, (tt % 2) * TS:(tt % 2) * TS + TS],
                sel_bf[0:2, 128:256], rows2[:, tt * TS:(tt + 1) * TS],
                start=True, stop=True), "pe")
        p.wait("vector", "pe", vmm)
        p.wait("vector", "dve", p.sems["dve"])
        vcp = 0
        for h in range(NB2):
            p.op("vector", lambda e, hh=h: e.tensor_copy(
                scr(0)[:, hh * WH:(hh + 1) * WH], pb[hh][:, 0:WH]), "dve")
            vcp = p.op("vector", lambda e, hh=h: e.tensor_copy(
                scr(1)[:, hh * WH:(hh + 1) * WH], pb[2 + hh][:, 0:WH]), "dve")
        for b in range(8):
            bank_free[b] = ("dve", vcp)
        return vcp

    xbf = [Ay_bf[:, f * T:(f + 1) * T] for f in range(NF)]

    def cast_x(f):
        p.wait("scalar", xsem[f], vx[f])
        p.wait("scalar", "act", p.sems["act"])
        p.op("scalar", lambda e, ff=f: e.copy(xbf[ff], Ax[:, ff, :]), "act")

    for f in range(NF):
        p.wait("vector", xsem[f], vx[f])
    sq_bf = sq_bf_t[:, :]
    ra1, rb1, vrow = ln_stats(xbf, sq_bf, persist=True, pre=cast_x, vpe_guard=0)
    expand_rows(ra1, rb1, vrow)
    p.wait("vector", "sr", p.sems["sr"])
    p.wait("vector", "dve", v_mod_done)
    h1 = [Abf[:, f, :] for f in range(NF)]
    for f in range(NF):
        dve(lambda e, ff=f: e.tensor_sub(scr(2), Ax[:, ff, :], scr(0)))
        dve(lambda e: e.tensor_mul(scr(3), scr(2), scr(1)))
        dve(lambda e, ff=f: e.tensor_scalar(
            h1[ff], scr(3), CA1[:, ff:ff + 1], SA[:, ff:ff + 1], ALU.mult, ALU.add))
    v_h1 = p.sems["dve"]

    p.wait("tensor", "dve", max(v_h1, v_wqv1))
    p.wait("tensor", "dp", v_params)
    p.wait("tensor", "swb", 16)
    p.wait("tensor", "sg2", v_wq1g)
    for t in range(NT):
        bb = 2 * (t % 2)
        pe_claim([bb])
        vpe = 0
        for kc in range(8):
            vpe = p.op("tensor", lambda e, k=kc, b=bb, tt=t: e.matmul(
                bank(b)[0:80, 0:TS], wcomb[:, k * 80:(k + 1) * 80],
                h1[k][:, tt * TS:(tt + 1) * TS], start=(k == 0), stop=(k == 7)), "pe")
        p.wait("scalar", "pe", vpe)
        p.wait("scalar", "act", p.sems["act"])
        va = p.op("scalar", lambda e, tt=t, b=bb: e.activation(
            aux[64:80, tt * TS:(tt + 1) * TS], bank(b)[64:80, 0:TS],
            AF.Sigmoid, bias=bb16), "act")
        p.wait("vector", "pe", vpe)
        p.wait("vector", "dve", p.sems["dve"])
        vd = p.op("vector", lambda e, tt=t, b=bb: e.tensor_copy(
            aux[0:48, tt * TS:(tt + 1) * TS], bank(b)[0:48, 0:TS]), "dve")
        bank_free[bb] = [("dve", vd), ("act", va)]
    p.wait("vector", "act", p.sems["act"])
    p.wait("vector", "pe", p.sems["pe"])
    p.wait("vector", "dve", p.sems["dve"])
    p.op("vector", lambda e: e.tensor_copy(sq_bf_t[0:48, :], aux[0:48, :]), "dve")
    v_cbf = p.op("vector", lambda e: e.tensor_copy(sq_bf_t[64:80, :], aux[64:80, :]), "dve")

    p.wait("gpsimd", "pe", v_modpe)
    p.wait("gpsimd", "dve", vrq)
    wk_bf = Aw[:, 0:8 * D]
    v_wk = p.dma("gpsimd", w_chunks(wk_bf, 8), dWk.rearrange("(k p) c -> p k c", p=128), "swk")

    u = [Ay_bf[:, f * T:(f + 1) * T] for f in range(NF)]
    p.wait("tensor", "swk", 16)
    p.wait("tensor", "dve", v_cbf)
    NB512 = max(1, WH // 512)
    itA = 0
    for f in range(NF):
        for hh in range(NB2):
            pk = 0 if itA % 2 == 0 else 2
            pv = 1 if itA % 2 == 0 else 3
            vsl = itA % 2
            kb = [2 * pk, 2 * pk + 1][:NB512]
            vbk = [2 * pv, 2 * pv + 1][:NB512]
            pe_claim(vbk)
            vv = 0
            for th in range(NTH2):
                tt = hh * NTH2 + th
                vv = p.op("tensor", lambda e, ff=f, t_=tt, o=th, pv_=pv: e.matmul(
                    pb[pv_][:, o * TS:o * TS + TS], E3[0:16, ff * 128:(ff + 1) * 128],
                    sq_bf_t[0:16, t_ * TS:(t_ + 1) * TS], start=True, stop=True), "pe")
            pe_claim(kb)
            vk = 0
            for kc in range(8):
                for th in range(NTH2):
                    tt = hh * NTH2 + th
                    vk = p.op("tensor", lambda e, k=kc, ff=f, t_=tt, o=th, pk_=pk: e.matmul(
                        pb[pk_][:, o * TS:o * TS + TS],
                        w_chunks(wk_bf, 8)[:, k, ff * 128:(ff + 1) * 128],
                        h1[k][:, t_ * TS:(t_ + 1) * TS], start=(k == 0), stop=(k == 7)), "pe")
            p.wait("scalar", "pe", vv)
            p.wait("scalar", "act", p.sems["act"])
            p.wait("scalar", "dve", p.sems["dve"])
            vvs = p.op("scalar", lambda e, pv_=pv, s_=vsl: e.copy(
                scr(s_)[:, 0:WH], pb[pv_][:, 0:WH]), "act")
            for b in vbk:
                bank_free[b] = ("act", vvs)
            p.wait("vector", "pe", vk)
            p.wait("vector", "act", vvs)
            p.wait("vector", "dve", p.sems["dve"])
            vu = p.op("vector", lambda e, ff=f, h=hh, pk_=pk, s_=vsl: e.tensor_mul(
                u[ff][:, h * WH:h * WH + WH], pb[pk_][:, 0:WH], scr(s_)[:, 0:WH]), "dve")
            for b in kb:
                bank_free[b] = ("dve", vu)
            itA += 1
    o_bf = h1
    itB = 0
    for f in range(NF):
        for hh in range(NB2):
            pbe = 2 if itB % 2 == 0 else 0
            pq = 3 if itB % 2 == 0 else 1
            beb = [2 * pbe, 2 * pbe + 1][:NB512]
            qqb = [2 * pq, 2 * pq + 1][:NB512]
            pe_claim(beb)
            pe_claim(qqb)
            vb_ = vq = 0
            for th in range(NTH2):
                tt = hh * NTH2 + th
                vb_ = p.op("tensor", lambda e, ff=f, t_=tt, o=th, pp=pbe: e.matmul(
                    pb[pp][:, o * TS:o * TS + TS], E3[64:80, ff * 128:(ff + 1) * 128],
                    sq_bf_t[64:80, t_ * TS:(t_ + 1) * TS], start=True, stop=True), "pe")
                vq = p.op("tensor", lambda e, ff=f, t_=tt, o=th, pp=pq: e.matmul(
                    pb[pp][:, o * TS:o * TS + TS], E3[32:48, ff * 128:(ff + 1) * 128],
                    sq_bf_t[32:48, t_ * TS:(t_ + 1) * TS], start=True, stop=True), "pe")
            p.wait("vector", "pe", max(vb_, vq))
            p.wait("vector", "dve", p.sems["dve"])
            ini_m = 0.0 if hh == 0 else scr(2)[:, hh * WH - 1:hh * WH]
            p.op("vector", lambda e, ff=f, h=hh, im_=ini_m, pp=pbe: e.tensor_tensor_scan(
                scr(2)[:, h * WH:h * WH + WH], pb[pp][:, 0:WH], u[ff][:, h * WH:h * WH + WH],
                im_, ALU.mult, ALU.add), "dve")
            ini_b = 1.0 if hh == 0 else scr(3)[:, hh * WH - 1:hh * WH]
            p.wait("vector", "dve", p.sems["dve"])
            vsc = p.op("vector", lambda e, ff=f, h=hh, ib=ini_b, pp=pbe: e.tensor_tensor_scan(
                scr(3)[:, h * WH:h * WH + WH], pb[pp][:, 0:WH],
                u[ff][:, h * WH:h * WH + WH], ib, ALU.mult, ALU.bypass), "dve")
            for b in beb:
                bank_free[b] = ("dve", vsc)
            p.wait("vector", "dve", p.sems["dve"])
            p.op("vector", lambda e, ff=f, h=hh, pp=pq: e.tensor_mul(
                o_bf[ff][:, h * WH:h * WH + WH], scr(2)[:, h * WH:h * WH + WH],
                pb[pp][:, 0:WH]), "dve")
            p.wait("vector", "dve", p.sems["dve"])
            vqB = p.op("vector", lambda e, ff=f, h=hh, pp=pq: e.tensor_mul(
                u[ff][:, h * WH:h * WH + WH], scr(3)[:, h * WH:h * WH + WH],
                pb[pp][:, 0:WH]), "dve")
            for b in qqb:
                bank_free[b] = ("dve", vqB)
            itB += 1
        dve(lambda e, ff=f: e.tensor_copy(summ[:, ff:ff + 1], scr(2)[:, T - 1:T]))
        dve(lambda e, ff=f: e.tensor_copy(summ[:, NF + ff:NF + ff + 1], scr(3)[:, T - 1:T]))
    v_summ = p.sems["dve"]

    p.wait("gpsimd", "dve", v_summ)
    v_ci = p.dma("gpsimd", dcc_in.ap(), summ[:], "sci")
    p.wait("gpsimd", "sci", 16)
    v_cc = p.op("gpsimd", lambda e: e.collective_compute(
        "AllGather", ALU.bypass, replica_groups=[list(range(NCORES))],
        ins=[dcc_in.ap()], outs=[dcc_out.ap()]), "cc")
    p.wait("gpsimd", "cc", v_cc)
    v_cg = p.dma("gpsimd", gathsb[:], dcc_out.ap().rearrange("(j p) q -> p j q", p=128), "scg")
    mA, mB = minit[:, 0:NF], minit[:, NF:2 * NF]
    sA, sB = smalls[:, 336:336 + NF], smalls[:, 344:344 + NF]
    p.wait("vector", "scg", 16)
    p.wait("vector", "dp", v_params)
    dve(lambda e: e.memset(mA, 0.0))
    for j in range(NCORES - 1):
        pm = pmask_sb[:, j * NF:(j + 1) * NF]
        dve(lambda e, jj=j, m=pm: e.tensor_mul(sA, gathsb[:, jj, 0:NF], m))
        dve(lambda e, jj=j: e.tensor_scalar_add(sB, gathsb[:, jj, NF:2 * NF], -1.0))
        dve(lambda e, m=pm: e.tensor_mul(sB, sB, m))
        dve(lambda e: e.tensor_scalar_add(sB, sB, 1.0))
        dve(lambda e: e.tensor_mul(mB, mA, sB))
        dve(lambda e: e.tensor_add(mA, mB, sA))
    v_minit = p.sems["dve"]

    for f in range(NF):
        dve(lambda e, ff=f: e.scalar_tensor_tensor(
            o_bf[ff], u[ff], minit[:, ff:ff + 1], o_bf[ff], ALU.mult, ALU.add))
    v_o = p.sems["dve"]

    wo_bf = Aw[:, 0:8 * D]
    p.wait("gpsimd", "pe", p.sems["pe"])
    v_wo = p.dma("gpsimd", w_chunks(wo_bf, 8), dWo.rearrange("(k p) c -> p k c", p=128), "swo")
    expand_rows2(pr_rows, p.sems["dve"])
    ybf = [Ay_bf[:, f * T:(f + 1) * T] for f in range(NF)]
    p.wait("tensor", "dve", v_o)
    p.wait("tensor", "swo", 16)
    p.wait("tensor", "sbo", 16)
    for f in range(NF):
        kb = [(0, 1), (2, 3)][f % 2]
        kbanks = [2 * kb[0], 2 * kb[0] + 1, 2 * kb[1], 2 * kb[1] + 1][:max(NT, 1)]
        pe_claim(kbanks)
        vpe = 0
        for t in range(NT):
            out_ap = pb[kb[t // 2]][:, (t % 2) * TS:(t % 2) * TS + TS]
            p.op("tensor", lambda e, ff=f, o=out_ap: e.matmul(
                o, bo_row_bf[0:1, ff * 128:(ff + 1) * 128], ones_row_bf[0:1, 0:TS],
                start=True, stop=False), "pe")
            for kc in range(8):
                vpe = p.op("tensor", lambda e, k=kc, ff=f, tt=t, o=out_ap: e.matmul(
                    o, w_chunks(wo_bf, 8)[:, k, ff * 128:(ff + 1) * 128],
                    o_bf[k][:, tt * TS:(tt + 1) * TS], start=False, stop=(k == 7)), "pe")
        dve(lambda e, ff=f: e.tensor_sub(scr(2), Ax[:, ff, :], scr(0)))
        dve(lambda e: e.tensor_mul(scr(3), scr(2), scr(1)))
        dve(lambda e, ff=f: e.tensor_scalar(
            scr(2), scr(3), CA1[:, ff:ff + 1], SA[:, ff:ff + 1], ALU.mult, ALU.add))
        p.wait("vector", "pe", vpe)
        p.wait("vector", "dve", p.sems["dve"])
        vy = 0
        for hh in range(NH2):
            vy = p.op("vector", lambda e, ff=f, h=hh, b0=kb: e.tensor_add(
                ybf[ff][:, h * W1:h * W1 + W1],
                scr(2)[:, h * W1:h * W1 + W1], pb[b0[h]][:, 0:W1]), "dve")
        for b in kbanks:
            bank_free[b] = ("dve", vy)
    v_y = p.sems["dve"]

    p.wait("scalar", "dve", v_y)
    ray, rby, vrow = ln_stats(ybf, sq_bf, persist=False)
    expand_rows(ray, rby, vrow)
    for f in range(NF):
        dve(lambda e, ff=f: e.tensor_sub(scr(2), ybf[ff], scr(0)))
        dve(lambda e: e.tensor_mul(scr(3), scr(2), scr(1)))
        dve(lambda e, ff=f: e.tensor_scalar(
            scr(2), scr(3), GAG[:, ff:ff + 1], GAB[:, ff:ff + 1], ALU.mult, ALU.add))
        dve(lambda e, ff=f: e.tensor_add(Ax[:, ff, :], Ax[:, ff, :], scr(2)))
    v_xa = p.sems["dve"]

    x2bf = [Ay_bf[:, f * T:(f + 1) * T] for f in range(NF)]
    p.wait("vector", "dve", v_xa)
    p.wait("vector", "pe", p.sems["pe"])
    p.wait("vector", "act", p.sems["act"])
    for f in range(NF):
        p.op("vector", lambda e, ff=f: e.tensor_copy(x2bf[ff], Ax[:, ff, :]), "dve")
    v_x2bf = p.sems["dve"]
    p.wait("scalar", "dve", v_x2bf)
    ra2, rb2, vrow = ln_stats(x2bf, sq_bf, persist=False)
    expand_rows(ra2, rb2, vrow)
    h2 = [Abf[:, f, :] for f in range(NF)]
    p.wait("vector", "pe", p.sems["pe"])
    pe_claim(list(range(NT)))
    p.wait("tensor", "dp", v_params)
    vg = 0
    for f in range(NF):
        dve(lambda e, ff=f: e.tensor_sub(scr(2), Ax[:, ff, :], scr(0)))
        dve(lambda e: e.tensor_mul(scr(3), scr(2), scr(1)))
        dve(lambda e, ff=f: e.tensor_scalar(
            scr(2), scr(3), CB1[:, ff:ff + 1], SB_[:, ff:ff + 1], ALU.mult, ALU.add))
        vh2 = dve(lambda e, ff=f: e.tensor_scalar(
            h2[ff], scr(3), CB1[:, ff:ff + 1], SB_[:, ff:ff + 1], ALU.mult, ALU.add))
        p.wait("tensor", "dve", vh2)
        for t in range(NT):
            vg = p.op("tensor", lambda e, ff=f, tt=t: e.matmul(
                bank(tt)[0:E, 0:TS], wg_sb[:, ff * E:(ff + 1) * E],
                scr(2)[:, tt * TS:(tt + 1) * TS], start=(ff == 0), stop=(ff == NF - 1)), "pe")
        if f < NF - 1:
            p.wait("vector", "pe", vg)
    g8 = scr(2)[0:8, :]
    p.wait("vector", "pe", p.sems["pe"])
    p.wait("vector", "act", p.sems["act"])
    p.wait("vector", "dve", p.sems["dve"])
    for t in range(NT):
        vge = p.op("vector", lambda e, tt=t: e.tensor_copy(
            g8[:, tt * TS:(tt + 1) * TS], bank(tt)[0:E, 0:TS]), "dve")
        bank_free[t] = ("dve", vge)
    tb = pb[3]
    pe_claim([6, 7])
    p.wait("tensor", "dve", max(p.sems["dve"], v_id))
    vtr = 0
    for c in range(TC):
        vtr = p.op("tensor", lambda e, cc_=c: e.transpose(
            tb[:, cc_ * E:(cc_ + 1) * E], g8[:, cc_ * 128:(cc_ + 1) * 128], ident[0:E, 0:E]), "pe")
    p.wait("vector", "pe", vtr)
    p.wait("vector", "dve", p.sems["dve"])
    p.op("vector", lambda e: e.tensor_copy(
        gtm[:].rearrange("p a b -> p (a b)"), tb[:, 0:TC * E]), "dve")
    bank_free[6] = ("dve", p.sems["dve"])
    bank_free[7] = ("dve", p.sems["dve"])

    dve(lambda e: e.reduce_max(st1[:, :, 0:1], gtm[:], axis=mybir.AxisListType.X))
    for c in range(TC):
        dve(lambda e, cc_=c: e.tensor_scalar(
            st2[:, cc_, :], gtm[:, cc_, :], st1[:, cc_, 0:1], None, ALU.subtract))
    p.wait("scalar", "dve", p.sems["dve"])
    p.wait("scalar", "act", p.sems["act"])
    vex = p.op("scalar", lambda e: e.activation(st2[:], st2[:], AF.Exp), "act")
    p.wait("vector", "act", vex)
    dve(lambda e: e.reduce_sum(st1[:, :, 0:1], st2[:], axis=mybir.AxisListType.X))
    dve(lambda e: e.reciprocal(st1[:, :, 1:2], st1[:, :, 0:1]))
    for c in range(TC):
        dve(lambda e, cc_=c: e.tensor_scalar(
            gtm[:, cc_, :], st2[:, cc_, :], st1[:, cc_, 1:2], None, ALU.mult))
    dve(lambda e: e.reduce_max(st1[:, :, 0:1], gtm[:], axis=mybir.AxisListType.X))
    for c in range(TC):
        dve(lambda e, cc_=c: e.tensor_scalar(
            st2[:, cc_, :], gtm[:, cc_, :], st1[:, cc_, 0:1], None, ALU.is_ge))
    dve(lambda e: e.scalar_tensor_tensor(st2[:], st2[:], -1e30, gtm[:], ALU.mult, ALU.add))
    dve(lambda e: e.reduce_max(st1[:, :, 2:3], st2[:], axis=mybir.AxisListType.X))
    for c in range(TC):
        dve(lambda e, cc_=c: e.tensor_scalar(
            st2[:, cc_, :], gtm[:, cc_, :], st1[:, cc_, 2:3], None, ALU.is_ge))
    dve(lambda e: e.tensor_mul(wtm[:], gtm[:], st2[:]))
    dve(lambda e: e.tensor_add(st1[:, :, 3:4], st1[:, :, 0:1], st1[:, :, 2:3]))
    dve(lambda e: e.tensor_scalar_add(st1[:, :, 3:4], st1[:, :, 3:4], 1e-8))
    dve(lambda e: e.reciprocal(st1[:, :, 4:5], st1[:, :, 3:4]))
    for c in range(TC):
        dve(lambda e, cc_=c: e.tensor_scalar(
            wtm[:, cc_, :], wtm[:, cc_, :], st1[:, cc_, 4:5], None, ALU.mult))
    v_wtm = p.sems["dve"]
    wTs = sq_bf_t[0:8, :]
    pe_claim([6, 7])
    p.wait("tensor", "dve", v_wtm)
    bs_tr = min(8, TC)
    for half in range(TC // bs_tr):
        vtr = 0
        for c8 in range(bs_tr):
            cc = half * bs_tr + c8
            vtr = p.op("tensor", lambda e, cc_=cc, j=c8: e.transpose(
                tb[0:E, j * 128:(j + 1) * 128], wtm[:, cc_, :], ident), "pe")
        p.wait("vector", "pe", vtr)
        p.wait("vector", "dve", p.sems["dve"])
        p.op("vector", lambda e, h=half, w_=bs_tr * 128: e.tensor_copy(
            wTs[:, h * w_:(h + 1) * w_], tb[0:E, 0:w_]), "dve")
        if half + 1 < TC // bs_tr:
            p.wait("tensor", "dve", p.sems["dve"])
    bank_free[6] = ("dve", p.sems["dve"])
    bank_free[7] = ("dve", p.sems["dve"])
    v_wT_dve = p.sems["dve"]

    slots = [Aw[:, i * 2048:(i + 1) * 2048] for i in range(6)]
    slot_state = [None] * 6
    slot_sems = [f"r{i}" for i in range(6)]
    slot_loads = [0] * 6
    p.wait("gpsimd", "pe", p.sems["pe"])

    def load_slab(slot, in_ap, a):
        if slot_state[slot] is not None:
            p.wait("gpsimd", *slot_state[slot])
            slot_state[slot] = None
        slot_loads[slot] += 1
        return p.dma("gpsimd", w_chunks(slots[slot], a), in_ap, slot_sems[slot])

    NB = max(1, T // 1024)
    NTH = max(1, W1 // TS)
    NU = 8 * NB
    wexp_b = Ay_bf[:, 0:T]
    t1_t = [Ay_bf[:, T + i * W1:T + (i + 1) * W1] for i in range(2)]
    s_bf = [Ay_bf[:, T + 4 * W1:T + 5 * W1] for i in range(1)]
    F_t = [Ascr_bf[:, i * T:(i + 1) * T] for i in range(8)]
    gv_slot_seq = [(2, 3), (4, 5), (0, 1), (2, 3)]
    w2_slots = [4, 5, 0, 1]
    wT_bf = sq_bf_t[0:8, :]

    def unit_ap(u, th=None):
        pt = pb[u % 4]
        if th is None:
            return pt[:, 0:W1]
        return pt[:, th * TS:th * TS + TS]

    def unit_banks(u):
        return [2 * (u % 4) + i for i in range(max(1, W1 // 512))]

    def emit_V(pend):
        ui, itl, tp, im, s12, s12_wait = pend
        pe_claim(unit_banks(ui))
        p.wait("tensor", slot_sems[s12], s12_wait)
        vv = 0
        for kc in range(8):
            for th in range(NTH):
                tt = tp * NTH + th
                vv = p.op("tensor", lambda e, k=kc, s_=s12, m=im, uu=ui, hh=th, t_=tt: e.matmul(
                    unit_ap(uu, hh), w_chunks(slots[s_], 8)[:, k, m * 128:(m + 1) * 128],
                    h2[k][:, t_ * TS:(t_ + 1) * TS],
                    start=(k == 0), stop=(k == 7)), "pe")
        if im == 1 and tp == NB - 1:
            slot_state[s12] = ("pe", vv)
        p.wait("vector", "pe", vv)
        p.wait("vector", "dve", p.sems["dve"])
        vF = p.op("vector", lambda e, uu=ui, ff=itl, t0=tp: e.tensor_mul(
            F_t[ff][:, t0 * W1:t0 * W1 + W1], t1_t[uu % 2][:, 0:W1], unit_ap(uu)), "dve")
        for b in unit_banks(ui):
            bank_free[b] = ("dve", vF)

    p.wait("tensor", "dve", v_wT_dve)
    v_wexp_used = 0
    for ei in range(E):
        wbanks = [4 + i for i in range(max(1, (2 * T) // 1024))][:4]
        pe_claim(wbanks)
        p.wait("tensor", "dve", v_wexp_used)
        p.wait("tensor", "dve", max(v_sel, v_wT_dve))
        vwe_pe = 0
        for t in range(NT):
            vwe_pe = p.op("tensor", lambda e, tt=t, ee=ei: e.matmul(
                pb[2 + tt // 2][:, (tt % 2) * TS:(tt % 2) * TS + TS] if T > 512 else pb[2][:, 0:TS],
                sel_bf[:, ee * 128:(ee + 1) * 128], wT_bf[0:8, tt * TS:(tt + 1) * TS],
                start=True, stop=True), "pe")
        p.wait("vector", "pe", vwe_pe)
        p.wait("vector", "dve", p.sems["dve"])
        vwe = 0
        for hh in range(NB):
            vwe = p.op("vector", lambda e, h=hh: e.tensor_copy(
                wexp_b[:, h * W1:h * W1 + W1],
                pb[2 + h][:, 0:W1] if T > 512 else pb[2][:, 0:W1]), "dve")
        for b in wbanks:
            bank_free[b] = ("dve", vwe)
        for ih in range(2):
            pend = None
            for it2 in range(4):
                s11, s12 = gv_slot_seq[it2]
                base_i = ih * 1024 + it2 * 256
                load_slab(s11, dW11[ei, :, base_i:base_i + 256].rearrange("(k p) c -> p k c", p=128), 8)
                load_slab(s12, dW12[ei, :, base_i:base_i + 256].rearrange("(k p) c -> p k c", p=128), 8)
                for im in range(2):
                    itl = it2 * 2 + im
                    for tp in range(NB):
                        ui = itl * NB + tp
                        pe_claim(unit_banks(ui))
                        p.wait("tensor", slot_sems[s11], 16 * slot_loads[s11])
                        vg = 0
                        for kc in range(8):
                            for th in range(NTH):
                                tt = tp * NTH + th
                                vg = p.op("tensor", lambda e, k=kc, s_=s11, m=im, uu=ui, hh=th, t_=tt: e.matmul(
                                    unit_ap(uu, hh), w_chunks(slots[s_], 8)[:, k, m * 128:(m + 1) * 128],
                                    h2[k][:, t_ * TS:(t_ + 1) * TS],
                                    start=(k == 0), stop=(k == 7)), "pe")
                        if im == 1 and tp == NB - 1:
                            slot_state[s11] = ("pe", vg)
                        p.wait("scalar", "pe", vg)
                        p.wait("scalar", "dve", p.sems["dve"])
                        p.wait("scalar", "act", p.sems["act"])
                        vs = p.op("scalar", lambda e, uu=ui: e.activation(
                            s_bf[0][:, 0:W1], unit_ap(uu), AF.Sigmoid), "act")
                        p.wait("vector", "act", vs)
                        p.wait("vector", "dve", p.sems["dve"])
                        vt1a = p.op("vector", lambda e, uu=ui: e.tensor_mul(
                            t1_t[uu % 2][:, 0:W1], unit_ap(uu), s_bf[0][:, 0:W1]), "dve")
                        p.wait("vector", "dve", vt1a)
                        vt1b = p.op("vector", lambda e, uu=ui, t0=tp: e.tensor_mul(
                            t1_t[uu % 2][:, 0:W1], t1_t[uu % 2][:, 0:W1],
                            wexp_b[:, t0 * W1:t0 * W1 + W1]), "dve")
                        for b in unit_banks(ui):
                            bank_free[b] = ("dve", vt1a)
                        if pend is not None:
                            emit_V(pend)
                        pend = (ui, itl, tp, im, s12, 16 * slot_loads[s12])
            emit_V(pend)
            pend = None
            v_F = p.sems["dve"]
            for ic2 in range(4):
                s2 = w2_slots[ic2]
                base_ic = ih * 1024 + ic2 * 256
                load_slab(s2, dW2[ei, base_ic:base_ic + 256, :].rearrange("(a p) c -> p a c", p=128), 2)
            for s2 in w2_slots:
                p.wait("tensor", slot_sems[s2], 16 * slot_loads[s2])
            p.wait("tensor", "dve", v_F)
            last_pe = 0
            for f in range(NF):
                obs = [4 * (f % 2) + t for t in range(NT)]
                pe_claim(obs)
                vpe = 0
                for ic in range(8):
                    for t in range(NT):
                        vpe = p.op("tensor", lambda e, i0=ic, ff=f, tt=t, b=obs[t]: e.matmul(
                            bank(b)[:, 0:TS],
                            w_chunks(slots[w2_slots[i0 // 2]], 2)[:, i0 % 2, ff * 128:(ff + 1) * 128],
                            F_t[i0][:, tt * TS:(tt + 1) * TS],
                            start=(i0 == 0), stop=(i0 == 7)), "pe")
                p.wait("vector", "pe", vpe)
                p.wait("vector", "dve", p.sems["dve"])
                vev = 0
                for t in range(NT):
                    vev = p.op("vector", lambda e, ff=f, tt=t, b=obs[t]: e.scalar_tensor_tensor(
                        Ax[:, ff, tt * TS:(tt + 1) * TS], bank(b)[:, 0:TS],
                        GB[:, ff:ff + 1], Ax[:, ff, tt * TS:(tt + 1) * TS],
                        ALU.mult, ALU.add), "dve")
                for b in obs:
                    bank_free[b] = ("dve", vev)
                last_pe = vpe
            for ic2 in range(4):
                slot_state[w2_slots[ic2]] = ("pe", last_pe)
        v_wexp_used = p.sems["dve"]

    v_out = p.sems["dve"]
    p.wait("sync", "dve", v_out)
    for f in range(NF):
        p.dma("sync", dy[f * 128:(f + 1) * 128, :], Ax[:, f, :], "dst")
    p.wait("sync", "dst", p.sems["dst"])

    for name in p.sems:
        p.sem_handles[name] = es.enter_context(nc.semaphore(name))
    with es:
        block = es.enter_context(nc.Block())

        @block.sync
        def _(e):
            for fn in p.thunks["sync"]:
                fn(e)

        @block.gpsimd
        def _(e):
            for fn in p.thunks["gpsimd"]:
                fn(e)

        @block.tensor
        def _(e):
            for fn in p.thunks["tensor"]:
                fn(e)

        @block.vector
        def _(e):
            for fn in p.thunks["vector"]:
                fn(e)

        @block.scalar
        def _(e):
            for fn in p.thunks["scalar"]:
                fn(e)

    return nc


_cache = {}


def kernel(L0_attn_x, L0_attn_Wq, L0_attn_Wk, L0_attn_Wv, L0_attn_Wbeta,
           L0_attn_b_beta, L0_attn_Wo, L0_attn_b_o, L0_attn_ln_g, L0_attn_ln_b,
           L0_ffn_Wg, L0_ffn_W11, L0_ffn_W12, L0_ffn_W2, L0_c, L0_wmod):
    x = np.asarray(L0_attn_x, dtype=np.float32)
    B, S, _ = x.shape
    assert B == 1 and S % NCORES == 0
    T = S // NCORES
    if "nc" not in _cache or _cache.get("T") != T:
        _cache["nc"] = build_program(T=T, TS=min(512, T))
        _cache["T"] = T
    nc = _cache["nc"]

    Wq_full = np.ascontiguousarray(L0_attn_Wq, np.float32)
    Wv_full = np.ascontiguousarray(L0_attn_Wv, np.float32)
    wmod_full = np.ascontiguousarray(L0_wmod, np.float32)
    shared = {
        "Wk": np.ascontiguousarray(L0_attn_Wk, np.float32),
        "Wbeta": np.ascontiguousarray(L0_attn_Wbeta, np.float32),
        "b_beta": np.ascontiguousarray(L0_attn_b_beta, np.float32),
        "Wo": np.ascontiguousarray(L0_attn_Wo, np.float32),
        "b_o": np.ascontiguousarray(L0_attn_b_o, np.float32),
        "ln_g": np.ascontiguousarray(L0_attn_ln_g, np.float32),
        "ln_b": np.ascontiguousarray(L0_attn_ln_b, np.float32),
        "Wg": np.ascontiguousarray(L0_ffn_Wg, np.float32),
        "W11": np.ascontiguousarray(L0_ffn_W11, np.float32),
        "W12": np.ascontiguousarray(L0_ffn_W12, np.float32),
        "W2": np.ascontiguousarray(L0_ffn_W2, np.float32),
        "cvec": np.ascontiguousarray(L0_c, np.float32),
    }
    MSH = (6 * D) // NCORES
    in_maps = []
    for cid in range(NCORES):
        xT = np.ascontiguousarray(x[0, cid * T:(cid + 1) * T, :].T)
        import ml_dtypes
        pmask = np.zeros((128, NCORES * NF), ml_dtypes.bfloat16)
        for j in range(NCORES):
            if j < cid:
                pmask[:, j * NF:(j + 1) * NF] = 1.0
        in_maps.append({**shared, "xT": xT, "pmask": pmask,
                        "Wq": np.ascontiguousarray(Wq_full[cid * 128:(cid + 1) * 128, :]),
                        "Wv": np.ascontiguousarray(Wv_full[cid * 128:(cid + 1) * 128, :]),
                        "wmod": np.ascontiguousarray(wmod_full[:, cid * MSH:(cid + 1) * MSH])})

    res = run_bass_kernel_spmd(nc, in_maps, core_ids=list(range(NCORES)))
    out = np.empty((1, S, D), np.float32)
    for cid in range(NCORES):
        out[0, cid * T:(cid + 1) * T, :] = res.results[cid]["yT"].T
    return out


# version 22
# speedup vs baseline: 1.0095x; 1.0095x over previous
import numpy as np
from contextlib import ExitStack

import concourse.bass as bass
import concourse.mybir as mybir
from concourse.bass_utils import run_bass_kernel_spmd

F32 = mybir.dt.float32
F32R = mybir.dt.float32r
BF16 = mybir.dt.bfloat16
AF = mybir.ActivationFunctionType
ALU = mybir.AluOpType

NCORES = 8
D = 1024
NH = 16
HD = 64
E = 8
II = 2048
NF = D // 128
EPS = 1e-5


class Prog:
    ENGS = ("sync", "gpsimd", "tensor", "vector", "scalar")

    def __init__(self):
        self.thunks = {e: [] for e in self.ENGS}
        self.sems = {}
        self.sem_handles = {}
        self.waited = {e: {} for e in self.ENGS}

    def declare(self, *names):
        for n in names:
            self.sems[n] = 0

    def wait(self, eng, sem, val):
        assert val <= self.sems[sem], f"wait {sem}>={val} > emitted {self.sems[sem]}"
        if val <= self.waited[eng].get(sem, 0):
            return
        self.waited[eng][sem] = val
        self.thunks[eng].append(lambda e, s=sem, v=val: e.wait_ge(self.sem_handles[s], v))

    def op(self, eng, build, inc=None, amt=1):
        if inc is None:
            self.thunks[eng].append(build)
            return None
        self.sems[inc] += amt
        self.thunks[eng].append(
            lambda e, b=build, s=inc: b(e).then_inc(self.sem_handles[s], amt))
        return self.sems[inc]

    def dma(self, eng, out, in_, inc, slow=False):
        if slow:
            return self.op(eng, lambda e: e.dma_start(
                out=out, in_=in_, allow_slow_non_contiguous=True), inc, 16)
        return self.op(eng, lambda e: e.dma_start(out=out, in_=in_), inc, 16)


def build_program(T=2048, TS=512):
    NT = T // TS
    TC = T // 128
    NH2 = max(1, NT // 2)
    W1 = min(1024, T)
    nc = bass.Bass("TRN2", target_bir_lowering=False)
    p = Prog()
    p.declare("dp", "dxx", "dxg", "dst", "swk", "swo", "sbo", "sci", "scg", "swb", "sr",
              "pe", "dve", "act", "gpc", "cc",
              *[f"r{i}" for i in range(6)], *[f"m{i}" for i in range(4)],
              "q0", "q1", "e0", "e1", "e2", "e3", "e4", "smod", "sg2")

    dx = nc.dram_tensor("xT", [D, T], F32, kind="ExternalInput").ap()
    dWq = nc.dram_tensor("Wq", [128, D], F32, kind="ExternalInput").ap()
    dWk = nc.dram_tensor("Wk", [D, D], F32, kind="ExternalInput").ap()
    dWv = nc.dram_tensor("Wv", [128, D], F32, kind="ExternalInput").ap()
    dWbeta = nc.dram_tensor("Wbeta", [D, NH], F32, kind="ExternalInput").ap()
    db_beta = nc.dram_tensor("b_beta", [NH], F32, kind="ExternalInput").ap()
    dWo = nc.dram_tensor("Wo", [D, D], F32, kind="ExternalInput").ap()
    db_o = nc.dram_tensor("b_o", [D], F32, kind="ExternalInput").ap()
    dln_g = nc.dram_tensor("ln_g", [D], F32, kind="ExternalInput").ap()
    dln_b = nc.dram_tensor("ln_b", [D], F32, kind="ExternalInput").ap()
    dWg = nc.dram_tensor("Wg", [D, E], F32, kind="ExternalInput").ap()
    dW11 = nc.dram_tensor("W11", [E, D, II], F32, kind="ExternalInput").ap()
    dW12 = nc.dram_tensor("W12", [E, D, II], F32, kind="ExternalInput").ap()
    dW2 = nc.dram_tensor("W2", [E, II, D], F32, kind="ExternalInput").ap()
    dc = nc.dram_tensor("cvec", [1, D], F32, kind="ExternalInput").ap()
    dwmod = nc.dram_tensor("wmod", [D, (6 * D) // NCORES], F32, kind="ExternalInput").ap()
    dpmask = nc.dram_tensor("pmask", [128, NCORES * NF], BF16, kind="ExternalInput").ap()
    dy = nc.dram_tensor("yT", [D, T], F32, kind="ExternalOutput").ap()
    dcc_in = nc.dram_tensor("cc_in", [128, 2 * NF], F32)
    dcc_out = nc.dram_tensor("cc_out", [NCORES * 128, 2 * NF], F32)

    es = ExitStack()
    sbuf = lambda n, sh, dt: es.enter_context(nc.sbuf_tensor(n, sh, dt))

    Ax = sbuf("Ax", [128, NF, T], F32)
    Ascr = sbuf("Ascr", [128, 4 * T], F32)
    Abf = sbuf("Abf", [128, NF, T], BF16)
    Ay = sbuf("Ay", [128, 4 * T], F32)
    Aw = sbuf("Aw", [128, 6 * 2048], BF16)
    aux = sbuf("aux", [128, T], F32)
    comp = aux[0:48, :]
    rows = aux[:, :]
    sq_bf_t = sbuf("sqbf", [128, T], BF16)
    mod_cols = sbuf("mod_cols", [128, 10 * NF], F32)
    smalls = sbuf("smalls", [128, 448], F32)
    smallsb = sbuf("smallsb", [128, 32 + TS + D + 2048], BF16)
    wcomb = sbuf("wcomb", [128, NF * 80], BF16)
    wg_sb = sbuf("wg_sb", [128, NF * E], F32)
    gathsb = sbuf("gathsb", [128, NCORES, 2 * NF], F32)
    summ = sbuf("summ", [128, 2 * NF], F32)
    minit = sbuf("minit", [128, 2 * NF], F32)
    pmask_sb = sbuf("pmask_sb", [128, NCORES * NF], BF16)

    Ascr_bf = Ascr[:].bitcast(BF16)
    gtm = Ay[:, 0:TC * E].rearrange("p (a b) -> p a b", b=E)
    wtm = Ay[:, TC * E:2 * TC * E].rearrange("p (a b) -> p a b", b=E)
    st2 = Ay[:, 2 * TC * E:3 * TC * E].rearrange("p (a b) -> p a b", b=E)
    st1 = Ay[:, 3 * TC * E:3 * TC * E + 5 * TC].rearrange("p (a b) -> p a b", b=5)
    Ay_bf = Ay[:].bitcast(BF16)
    Aw_f32 = Aw[:].bitcast(F32)

    pb = [es.enter_context(nc.psum_tensor(f"pb{i}", [128, 1024], F32)) for i in range(4)]

    def bank(i):
        return pb[i // 2][:, (i % 2) * 512:(i % 2) * 512 + 512]

    pr_rows = Aw[0:2, 8192:8192 + T]
    bank_free = {}

    def pe_claim(banks):
        for b in banks:
            st = bank_free.pop(b, None)
            if st is not None:
                if isinstance(st, list):
                    for sem, val in st:
                        p.wait("tensor", sem, val)
                else:
                    p.wait("tensor", st[0], st[1])

    def scr(i):
        return Ascr[:, i * T:(i + 1) * T]

    def w_chunks(view, nk):
        return view.rearrange("p (k c) -> p k c", k=nk)

    c_col = smalls[:, 0:NF]
    p.dma("sync", c_col, dc.rearrange("1 (a b) -> b a", b=128), "dp", slow=True)
    lng_col = smalls[:, 16:16 + NF]
    lnb_col = smalls[:, 32:32 + NF]
    p.dma("sync", lng_col, dln_g.rearrange("(a b) -> b a", b=128), "dp", slow=True)
    p.dma("sync", lnb_col, dln_b.rearrange("(a b) -> b a", b=128), "dp", slow=True)
    bb16 = smalls[64:80, 48:49]
    p.dma("sync", bb16, db_beta.unsqueeze(1), "dp")
    v_wcz = p.op("vector", lambda e: e.memset(wcomb[:], 0.0), "dve")
    p.wait("gpsimd", "dve", v_wcz)
    p.dma("gpsimd", wcomb[:].rearrange("p (f h) -> p f h", h=80)[:, :, 64:80],
          dWbeta.rearrange("(f p) h -> p f h", p=128), "swb")
    p.dma("sync", wg_sb[:].rearrange("p (f e) -> p f e", e=E),
          dWg.rearrange("(f p) e -> p f e", p=128), "dp")
    p.dma("sync", pmask_sb[:], dpmask, "dp")
    v_params = p.sems["dp"]

    identr = smalls[:, 64:192]
    v_iota = p.op("gpsimd", lambda e: e.iota(identr, pattern=[[-1, 128]], base=0,
                                             channel_multiplier=1,
                                             allow_small_or_imprecise_dtypes=True), "gpc")
    ident = smalls[:, 192:320]
    p.wait("vector", "gpc", v_iota)
    p.op("vector", lambda e: e.tensor_single_scalar(ident, identr, 0.0, ALU.is_equal), "dve")
    v_id = p.sems["dve"]
    ones_col_bf = smallsb[:, 0:1]
    p.op("vector", lambda e: e.memset(ones_col_bf, 1.0), "dve")
    ones_row_bf = smallsb[0:1, 32:32 + TS]
    p.op("vector", lambda e: e.memset(ones_row_bf, 1.0), "dve")
    v_ones = p.sems["dve"]
    bo_row_bf = smallsb[0:1, 32 + TS:32 + TS + D]
    sel_bf = smallsb[0:8, 32 + TS + D:32 + TS + D + 1024]
    v_selr = p.op("gpsimd", lambda e: e.iota(sel_bf, pattern=[[1, 8], [0, 128]], base=0,
                                             channel_multiplier=-1,
                                             allow_small_or_imprecise_dtypes=True), "gpc")
    p.wait("vector", "gpc", v_selr)
    p.op("vector", lambda e: e.tensor_single_scalar(sel_bf, sel_bf, 0.0, ALU.is_equal), "dve")
    v_sel = p.sems["dve"]
    E3 = smallsb[:, 32 + TS + D + 1024:32 + TS + D + 2048]
    for b0 in (0, 32, 64):
        p.op("gpsimd", lambda e, bb=b0: e.iota(E3[bb:bb + 16, :], pattern=[[1, 16], [0, 64]],
                                               base=0, channel_multiplier=-1,
                                               allow_small_or_imprecise_dtypes=True), "gpc")
    v_e3r = p.sems["gpc"]
    p.wait("vector", "gpc", v_e3r)
    for b0 in (0, 32, 64):
        p.op("vector", lambda e, bb=b0: e.tensor_single_scalar(
            E3[bb:bb + 16, :], E3[bb:bb + 16, :], 0.0, ALU.is_equal), "dve")
    v_e3 = p.sems["dve"]
    v_sel2 = v_sel

    eps_col = smalls[:, 424:425]
    p.op("vector", lambda e: e.memset(eps_col, EPS), "dve")
    v_eps = p.sems["dve"]
    silu_c = smalls[:, 320:320 + NF]
    p.wait("scalar", "dp", v_params)
    v_siluc = p.op("scalar", lambda e: e.activation(silu_c, c_col, AF.Silu), "act")
    v_bob = p.dma("gpsimd", bo_row_bf, db_o.unsqueeze(0), "sbo")
    vx = []
    xsem = []
    for f in range(NF):
        sem = "dxx" if f % 2 == 0 else "dxg"
        xsem.append(sem)
        if f >= 2:
            p.wait("scalar", sem, vx[f - 2])
        vx.append(p.dma("scalar", Ax[:, f, :], dx[f * 128:(f + 1) * 128, :], sem))

    MSH = (6 * D) // NCORES
    dcc2_in = nc.dram_tensor("cc2_in", [MSH + 2 * 128 * NH], F32)
    dcc2_out = nc.dram_tensor("cc2_out", [NCORES, MSH + 2 * 128 * NH], F32)
    wm_view = [Aw_f32[:, i * MSH:(i + 1) * MSH] for i in range(4)]
    wm_state = [None] * 4
    p.wait("tensor", "act", v_siluc)
    pe_claim([4, 5])
    vmm = 0
    for kc in range(8):
        slot = kc % 4
        if wm_state[slot] is not None:
            p.wait("gpsimd", *wm_state[slot])
            wm_state[slot] = None
        vld = p.dma("gpsimd", wm_view[slot], dwmod[kc * 128:(kc + 1) * 128, :], f"m{slot}")
        p.wait("tensor", f"m{slot}", vld)
        for j2, (o0, w0) in enumerate(((0, 512), (512, MSH - 512))):
            vmm = p.op("tensor", lambda e, s_=slot, k=kc, o=o0, w=w0, j=j2: e.matmul(
                bank(4 + j)[0:1, 0:w], silu_c[:, k:k + 1],
                wm_view[s_][:, o:o + w], start=(k == 0), stop=(k == 7)), "pe")
        wm_state[slot] = ("pe", p.sems["pe"])
    v_modpe = p.sems["pe"]
    p.wait("vector", "pe", vmm)
    p.wait("vector", "dve", p.sems["dve"])
    p.op("vector", lambda e: e.tensor_copy(Ascr[0:1, 0:512], bank(4)[0:1, 0:512]), "dve")
    vev = p.op("vector", lambda e: e.tensor_copy(
        Ascr[32:33, 0:MSH - 512], bank(5)[0:1, 0:MSH - 512]), "dve")
    bank_free[4] = ("dve", vev)
    bank_free[5] = ("dve", vev)
    p.wait("sync", "dve", vev)
    p.dma("sync", dcc2_in.ap()[0:512].unsqueeze(0), Ascr[0:1, 0:512], "smod")
    p.dma("sync", dcc2_in.ap()[512:MSH].unsqueeze(0), Ascr[32:33, 0:MSH - 512], "smod")

    rq_view = [Aw_f32[:, 4 * MSH + i * D:4 * MSH + (i + 1) * D] for i in range(2)]
    rq_stage = scr(3)[:, 0:64]
    for i, dmat in enumerate((dWv, dWq)):
        vld = p.dma("gpsimd", rq_view[i], dmat, f"q{i}")
        p.wait("vector", f"q{i}", vld)
        p.wait("vector", "dve", p.sems["dve"])
        for h in range(NH):
            p.op("vector", lambda e, si_=i, hh=h: e.reduce_sum(
                rq_stage[:, si_ * NH + hh:si_ * NH + hh + 1],
                rq_view[si_][:, hh * HD:(hh + 1) * HD], axis=mybir.AxisListType.X), "dve")
    vrq = p.sems["dve"]
    p.wait("sync", "dve", vrq)
    p.dma("sync", dcc2_in.ap()[MSH:MSH + 2048].rearrange("(pp h) -> pp h", pp=128),
          rq_stage[:, 0:NH], "smod")
    v_in2 = p.dma("sync", dcc2_in.ap()[MSH + 2048:MSH + 4096].rearrange(
        "(pp h) -> pp h", pp=128), rq_stage[:, NH:2 * NH], "smod")
    p.wait("gpsimd", "smod", v_in2)
    v_cc2 = p.op("gpsimd", lambda e: e.collective_compute(
        "AllGather", ALU.bypass, replica_groups=[list(range(NCORES))],
        ins=[dcc2_in.ap()], outs=[dcc2_out.ap()]), "cc")
    p.wait("sync", "cc", v_cc2)
    MQ = MSH // 128
    for j in range(NCORES):
        v_mcols = p.dma("sync", mod_cols[:, j * MQ:(j + 1) * MQ],
                        dcc2_out.ap()[j:j + 1, 0:MSH].rearrange(
                            "a (q pp) -> (a pp) q", pp=128), "smod", slow=True)
    p.wait("gpsimd", "cc", v_cc2)
    p.dma("gpsimd", wcomb[:].rearrange("p (k s) -> p k s", s=80)[:, :, 0:NH],
          dcc2_out.ap()[:, MSH:MSH + 2048].rearrange("j (pp h) -> pp j h", pp=128), "sg2")
    v_wq1g = p.dma("gpsimd", wcomb[:].rearrange("p (k s) -> p k s", s=80)[:, :, 32:32 + NH],
                   dcc2_out.ap()[:, MSH + 2048:MSH + 4096].rearrange(
                       "j (pp h) -> pp j h", pp=128), "sg2")
    p.wait("vector", "smod", p.sems["smod"])
    p.wait("scalar", "smod", p.sems["smod"])
    p.wait("vector", "dve", p.sems["dve"])
    SA, CA1, GA, SB_, CB1, GB, GAG, GAB, SC1, SC2 = [
        mod_cols[:, i * NF:(i + 1) * NF] for i in range(10)]

    def dve(build):
        p.wait("vector", "dve", p.sems["dve"])
        return p.op("vector", build, "dve")

    def gp(build):
        p.wait("gpsimd", "gpc", p.sems["gpc"])
        return p.op("gpsimd", build, "gpc")

    def clamp_gate(g):
        dve(lambda e: e.tensor_single_scalar(SC1, g, 0.0, ALU.is_ge))
        dve(lambda e: e.tensor_scalar(SC1, SC1, 2.0, -1.0, ALU.mult, ALU.add))
        dve(lambda e: e.tensor_mul(SC2, g, SC1))
        dve(lambda e: e.tensor_scalar_max(SC2, SC2, 0.25))
        dve(lambda e: e.tensor_mul(g, SC2, SC1))

    clamp_gate(GA)
    clamp_gate(GB)
    dve(lambda e: e.tensor_scalar_add(CA1, CA1, 1.0))
    dve(lambda e: e.tensor_scalar_add(CB1, CB1, 1.0))
    p.wait("vector", "dp", v_params)
    dve(lambda e: e.tensor_mul(GAG, GA, lng_col))
    dve(lambda e: e.tensor_mul(GAB, GA, lnb_col))
    v_mod_done = p.sems["dve"]
    v_wqv1 = vrq

    def ln_stats(src, sq_bf, persist=False, pre=None, vpe_guard=None):
        vg = p.sems["pe"] if vpe_guard is None else vpe_guard
        if vg:
            p.wait("scalar", "pe", vg)
        sq2 = Ascr_bf[:, 6 * T:7 * T]
        pe_claim(list(range(2 * NT)))
        vpe_f = []
        for f in range(NF):
            if pre is not None:
                pre(f)
            p.wait("scalar", "act", p.sems["act"])
            if f >= 2:
                p.wait("scalar", "pe", vpe_f[f - 2])
            sq = sq_bf if f % 2 == 0 else sq2
            vsq = p.op("scalar", lambda e, ff=f, q=sq: e.activation(q, src[ff], AF.Square), "act")
            p.wait("tensor", "act", vsq)
            for t in range(NT):
                p.op("tensor", lambda e, ff=f, tt=t: e.matmul(
                    bank(tt)[0:1, 0:TS], ones_col_bf,
                    src[ff][:, tt * TS:(tt + 1) * TS],
                    start=(ff == 0), stop=(ff == NF - 1)), "pe")
                p.op("tensor", lambda e, ff=f, tt=t, q=sq: e.matmul(
                    bank(NT + tt)[0:1, 0:TS], ones_col_bf,
                    q[:, tt * TS:(tt + 1) * TS],
                    start=(ff == 0), stop=(ff == NF - 1)), "pe")
            vpe_f.append(p.sems["pe"])
        v_st = p.sems["pe"]
        ra = scr(2)[0:1, :]
        rb = scr(3)[0:1, :]
        rc = scr(0)[0:1, :]
        p.wait("scalar", "pe", v_st)
        p.wait("scalar", "act", p.sems["act"])
        p.wait("scalar", "dve", p.sems["dve"])
        for t in range(NT):
            p.op("scalar", lambda e, tt=t: e.activation(
                ra[:, tt * TS:(tt + 1) * TS], bank(tt)[0:1, 0:TS], AF.Copy,
                scale=1.0 / D), "act")
            va = p.op("scalar", lambda e, tt=t: e.activation(
                rb[:, tt * TS:(tt + 1) * TS], bank(NT + tt)[0:1, 0:TS], AF.Copy,
                scale=1.0 / D), "act")
            bank_free[t] = ("act", va)
            bank_free[NT + t] = ("act", va)
        p.wait("scalar", "act", p.sems["act"])
        vmu = p.op("scalar", lambda e: e.activation(rc, ra, AF.Square), "act")
        p.wait("vector", "act", vmu)
        p.wait("vector", "dve", p.sems["dve"])
        vvar = p.op("vector", lambda e: e.tensor_sub(rb, rb, rc), "dve")
        p.wait("scalar", "dve", max(vvar, v_eps))
        p.wait("scalar", "act", p.sems["act"])
        vsq2 = p.op("scalar", lambda e: e.activation(
            rc, rb, AF.Sqrt, bias=eps_col[0:1, 0:1]), "act")
        p.wait("vector", "act", vsq2)
        p.wait("vector", "dve", p.sems["dve"])
        vrec = p.op("vector", lambda e: e.reciprocal(rb, rc), "dve")
        if persist:
            p.wait("gpsimd", "dve", vrec)
            p.wait("gpsimd", "pe", p.sems["pe"])
            p.dma("gpsimd", pr_rows[0:1, :], ra, "sr")
            p.dma("gpsimd", pr_rows[1:2, :], rb, "sr")
        return ra, rb, vrec

    NB2 = max(1, T // 1024)
    NTH2 = max(1, min(1024, T) // TS)
    WH = min(1024, T)

    ones_f32row = smalls[0:1, 64:192]
    p.wait("vector", "dve", v_id)
    p.op("vector", lambda e: e.memset(ones_f32row[:, 0:128], 1.0), "dve")
    v_onesf = p.sems["dve"]

    def expand_rows(mu_row, r_row, vrow_dve):
        pe_claim(list(range(8)))
        p.wait("tensor", "dve", max(vrow_dve, v_onesf))
        vmm = 0
        for t in range(NT):
            vmm = p.op("tensor", lambda e, tt=t: e.matmul(
                pb[tt // 2][:, (tt % 2) * TS:(tt % 2) * TS + TS],
                ones_f32row[:, 0:128], mu_row[:, tt * TS:(tt + 1) * TS],
                start=True, stop=True), "pe")
            vmm = p.op("tensor", lambda e, tt=t: e.matmul(
                pb[2 + tt // 2][:, (tt % 2) * TS:(tt % 2) * TS + TS],
                ones_f32row[:, 0:128], r_row[:, tt * TS:(tt + 1) * TS],
                start=True, stop=True), "pe")
        p.wait("vector", "pe", vmm)
        p.wait("vector", "dve", p.sems["dve"])
        vcp = 0
        for h in range(NB2):
            p.op("vector", lambda e, hh=h: e.tensor_copy(
                scr(0)[:, hh * WH:(hh + 1) * WH], pb[hh][:, 0:WH]), "dve")
            vcp = p.op("vector", lambda e, hh=h: e.tensor_copy(
                scr(1)[:, hh * WH:(hh + 1) * WH], pb[2 + hh][:, 0:WH]), "dve")
        for b in range(8):
            bank_free[b] = ("dve", vcp)
        return vcp

    def expand_rows2(rows2, vdep):
        p.wait("tensor", "sr", p.sems["sr"])
        pe_claim(list(range(8)))
        p.wait("tensor", "dve", max(vdep, v_sel2))
        vmm = 0
        for t in range(NT):
            vmm = p.op("tensor", lambda e, tt=t: e.matmul(
                pb[tt // 2][:, (tt % 2) * TS:(tt % 2) * TS + TS],
                sel_bf[0:2, 0:128], rows2[:, tt * TS:(tt + 1) * TS],
                start=True, stop=True), "pe")
            vmm = p.op("tensor", lambda e, tt=t: e.matmul(
                pb[2 + tt // 2][:, (tt % 2) * TS:(tt % 2) * TS + TS],
                sel_bf[0:2, 128:256], rows2[:, tt * TS:(tt + 1) * TS],
                start=True, stop=True), "pe")
        p.wait("vector", "pe", vmm)
        p.wait("vector", "dve", p.sems["dve"])
        vcp = 0
        for h in range(NB2):
            p.op("vector", lambda e, hh=h: e.tensor_copy(
                scr(0)[:, hh * WH:(hh + 1) * WH], pb[hh][:, 0:WH]), "dve")
            vcp = p.op("vector", lambda e, hh=h: e.tensor_copy(
                scr(1)[:, hh * WH:(hh + 1) * WH], pb[2 + hh][:, 0:WH]), "dve")
        for b in range(8):
            bank_free[b] = ("dve", vcp)
        return vcp

    xbf = [Ay_bf[:, f * T:(f + 1) * T] for f in range(NF)]

    def cast_x(f):
        p.wait("scalar", xsem[f], vx[f])
        p.wait("scalar", "act", p.sems["act"])
        p.op("scalar", lambda e, ff=f: e.copy(xbf[ff], Ax[:, ff, :]), "act")

    p.wait("vector", "dxx", vx[NF - 2])
    p.wait("vector", "dxg", vx[NF - 1])
    sq_bf = sq_bf_t[:, :]
    ra1, rb1, vrow = ln_stats(xbf, sq_bf, persist=True, pre=cast_x, vpe_guard=0)
    expand_rows(ra1, rb1, vrow)
    p.wait("vector", "sr", p.sems["sr"])
    p.wait("vector", "dve", v_mod_done)
    h1 = [Abf[:, f, :] for f in range(NF)]
    for f in range(NF):
        dve(lambda e, ff=f: e.tensor_sub(scr(2), Ax[:, ff, :], scr(0)))
        dve(lambda e: e.tensor_mul(scr(3), scr(2), scr(1)))
        dve(lambda e, ff=f: e.tensor_scalar(
            h1[ff], scr(3), CA1[:, ff:ff + 1], SA[:, ff:ff + 1], ALU.mult, ALU.add))
    v_h1 = p.sems["dve"]

    p.wait("tensor", "dve", max(v_h1, v_wqv1))
    p.wait("tensor", "dp", v_params)
    p.wait("tensor", "swb", 16)
    p.wait("tensor", "sg2", v_wq1g)
    for t in range(NT):
        bb = 2 * (t % 2)
        pe_claim([bb])
        vpe = 0
        for kc in range(8):
            vpe = p.op("tensor", lambda e, k=kc, b=bb, tt=t: e.matmul(
                bank(b)[0:80, 0:TS], wcomb[:, k * 80:(k + 1) * 80],
                h1[k][:, tt * TS:(tt + 1) * TS], start=(k == 0), stop=(k == 7)), "pe")
        p.wait("scalar", "pe", vpe)
        p.wait("scalar", "act", p.sems["act"])
        va = p.op("scalar", lambda e, tt=t, b=bb: e.activation(
            aux[64:80, tt * TS:(tt + 1) * TS], bank(b)[64:80, 0:TS],
            AF.Sigmoid, bias=bb16), "act")
        p.wait("vector", "pe", vpe)
        p.wait("vector", "dve", p.sems["dve"])
        vd = p.op("vector", lambda e, tt=t, b=bb: e.tensor_copy(
            aux[0:48, tt * TS:(tt + 1) * TS], bank(b)[0:48, 0:TS]), "dve")
        bank_free[bb] = [("dve", vd), ("act", va)]
    p.wait("vector", "act", p.sems["act"])
    p.wait("vector", "pe", p.sems["pe"])
    p.wait("vector", "dve", p.sems["dve"])
    p.op("vector", lambda e: e.tensor_copy(sq_bf_t[0:48, :], aux[0:48, :]), "dve")
    v_cbf = p.op("vector", lambda e: e.tensor_copy(sq_bf_t[64:80, :], aux[64:80, :]), "dve")

    p.wait("gpsimd", "pe", v_modpe)
    p.wait("gpsimd", "dve", vrq)
    wk_bf = Aw[:, 0:8 * D]
    v_wk = p.dma("gpsimd", w_chunks(wk_bf, 8), dWk.rearrange("(k p) c -> p k c", p=128), "swk")

    u = [Ay_bf[:, f * T:(f + 1) * T] for f in range(NF)]
    p.wait("tensor", "swk", 16)
    p.wait("tensor", "dve", v_cbf)
    NB512 = max(1, WH // 512)
    itA = 0
    for f in range(NF):
        for hh in range(NB2):
            pk = 0 if itA % 2 == 0 else 2
            pv = 1 if itA % 2 == 0 else 3
            vsl = itA % 2
            kb = [2 * pk, 2 * pk + 1][:NB512]
            vbk = [2 * pv, 2 * pv + 1][:NB512]
            pe_claim(vbk)
            vv = 0
            for th in range(NTH2):
                tt = hh * NTH2 + th
                vv = p.op("tensor", lambda e, ff=f, t_=tt, o=th, pv_=pv: e.matmul(
                    pb[pv_][:, o * TS:o * TS + TS], E3[0:16, ff * 128:(ff + 1) * 128],
                    sq_bf_t[0:16, t_ * TS:(t_ + 1) * TS], start=True, stop=True), "pe")
            pe_claim(kb)
            vk = 0
            for kc in range(8):
                for th in range(NTH2):
                    tt = hh * NTH2 + th
                    vk = p.op("tensor", lambda e, k=kc, ff=f, t_=tt, o=th, pk_=pk: e.matmul(
                        pb[pk_][:, o * TS:o * TS + TS],
                        w_chunks(wk_bf, 8)[:, k, ff * 128:(ff + 1) * 128],
                        h1[k][:, t_ * TS:(t_ + 1) * TS], start=(k == 0), stop=(k == 7)), "pe")
            p.wait("scalar", "pe", vv)
            p.wait("scalar", "act", p.sems["act"])
            p.wait("scalar", "dve", p.sems["dve"])
            vvs = p.op("scalar", lambda e, pv_=pv, s_=vsl: e.copy(
                scr(s_)[:, 0:WH], pb[pv_][:, 0:WH]), "act")
            for b in vbk:
                bank_free[b] = ("act", vvs)
            p.wait("vector", "pe", vk)
            p.wait("vector", "act", vvs)
            p.wait("vector", "dve", p.sems["dve"])
            vu = p.op("vector", lambda e, ff=f, h=hh, pk_=pk, s_=vsl: e.tensor_mul(
                u[ff][:, h * WH:h * WH + WH], pb[pk_][:, 0:WH], scr(s_)[:, 0:WH]), "dve")
            for b in kb:
                bank_free[b] = ("dve", vu)
            itA += 1
    o_bf = h1
    itB = 0
    for f in range(NF):
        for hh in range(NB2):
            pbe = 2 if itB % 2 == 0 else 0
            pq = 3 if itB % 2 == 0 else 1
            beb = [2 * pbe, 2 * pbe + 1][:NB512]
            qqb = [2 * pq, 2 * pq + 1][:NB512]
            pe_claim(beb)
            pe_claim(qqb)
            vb_ = vq = 0
            for th in range(NTH2):
                tt = hh * NTH2 + th
                vb_ = p.op("tensor", lambda e, ff=f, t_=tt, o=th, pp=pbe: e.matmul(
                    pb[pp][:, o * TS:o * TS + TS], E3[64:80, ff * 128:(ff + 1) * 128],
                    sq_bf_t[64:80, t_ * TS:(t_ + 1) * TS], start=True, stop=True), "pe")
                vq = p.op("tensor", lambda e, ff=f, t_=tt, o=th, pp=pq: e.matmul(
                    pb[pp][:, o * TS:o * TS + TS], E3[32:48, ff * 128:(ff + 1) * 128],
                    sq_bf_t[32:48, t_ * TS:(t_ + 1) * TS], start=True, stop=True), "pe")
            p.wait("vector", "pe", max(vb_, vq))
            p.wait("vector", "dve", p.sems["dve"])
            ini_m = 0.0 if hh == 0 else scr(2)[:, hh * WH - 1:hh * WH]
            p.op("vector", lambda e, ff=f, h=hh, im_=ini_m, pp=pbe: e.tensor_tensor_scan(
                scr(2)[:, h * WH:h * WH + WH], pb[pp][:, 0:WH], u[ff][:, h * WH:h * WH + WH],
                im_, ALU.mult, ALU.add), "dve")
            ini_b = 1.0 if hh == 0 else scr(3)[:, hh * WH - 1:hh * WH]
            p.wait("vector", "dve", p.sems["dve"])
            vsc = p.op("vector", lambda e, ff=f, h=hh, ib=ini_b, pp=pbe: e.tensor_tensor_scan(
                scr(3)[:, h * WH:h * WH + WH], pb[pp][:, 0:WH],
                u[ff][:, h * WH:h * WH + WH], ib, ALU.mult, ALU.bypass), "dve")
            for b in beb:
                bank_free[b] = ("dve", vsc)
            p.wait("vector", "dve", p.sems["dve"])
            p.op("vector", lambda e, ff=f, h=hh, pp=pq: e.tensor_mul(
                o_bf[ff][:, h * WH:h * WH + WH], scr(2)[:, h * WH:h * WH + WH],
                pb[pp][:, 0:WH]), "dve")
            p.wait("vector", "dve", p.sems["dve"])
            vqB = p.op("vector", lambda e, ff=f, h=hh, pp=pq: e.tensor_mul(
                u[ff][:, h * WH:h * WH + WH], scr(3)[:, h * WH:h * WH + WH],
                pb[pp][:, 0:WH]), "dve")
            for b in qqb:
                bank_free[b] = ("dve", vqB)
            itB += 1
        dve(lambda e, ff=f: e.tensor_copy(summ[:, ff:ff + 1], scr(2)[:, T - 1:T]))
        dve(lambda e, ff=f: e.tensor_copy(summ[:, NF + ff:NF + ff + 1], scr(3)[:, T - 1:T]))
    v_summ = p.sems["dve"]

    p.wait("gpsimd", "dve", v_summ)
    v_ci = p.dma("gpsimd", dcc_in.ap(), summ[:], "sci")
    p.wait("gpsimd", "sci", 16)
    v_cc = p.op("gpsimd", lambda e: e.collective_compute(
        "AllGather", ALU.bypass, replica_groups=[list(range(NCORES))],
        ins=[dcc_in.ap()], outs=[dcc_out.ap()]), "cc")
    p.wait("gpsimd", "cc", v_cc)
    v_cg = p.dma("gpsimd", gathsb[:], dcc_out.ap().rearrange("(j p) q -> p j q", p=128), "scg")
    mA, mB = minit[:, 0:NF], minit[:, NF:2 * NF]
    sA, sB = smalls[:, 336:336 + NF], smalls[:, 344:344 + NF]
    p.wait("vector", "scg", 16)
    p.wait("vector", "dp", v_params)
    dve(lambda e: e.memset(mA, 0.0))
    for j in range(NCORES - 1):
        pm = pmask_sb[:, j * NF:(j + 1) * NF]
        dve(lambda e, jj=j, m=pm: e.tensor_mul(sA, gathsb[:, jj, 0:NF], m))
        dve(lambda e, jj=j: e.tensor_scalar_add(sB, gathsb[:, jj, NF:2 * NF], -1.0))
        dve(lambda e, m=pm: e.tensor_mul(sB, sB, m))
        dve(lambda e: e.tensor_scalar_add(sB, sB, 1.0))
        dve(lambda e: e.tensor_mul(mB, mA, sB))
        dve(lambda e: e.tensor_add(mA, mB, sA))
    v_minit = p.sems["dve"]

    for f in range(NF):
        dve(lambda e, ff=f: e.scalar_tensor_tensor(
            o_bf[ff], u[ff], minit[:, ff:ff + 1], o_bf[ff], ALU.mult, ALU.add))
    v_o = p.sems["dve"]

    wo_bf = Aw[:, 0:8 * D]
    p.wait("gpsimd", "pe", p.sems["pe"])
    v_wo = p.dma("gpsimd", w_chunks(wo_bf, 8), dWo.rearrange("(k p) c -> p k c", p=128), "swo")
    vexp5 = expand_rows2(pr_rows, p.sems["dve"])
    ybf = [Ay_bf[:, f * T:(f + 1) * T] for f in range(NF)]
    p.wait("gpsimd", "dve", max(v_o, vexp5))
    p.wait("tensor", "dve", v_o)
    p.wait("tensor", "swo", 16)
    p.wait("tensor", "sbo", 16)
    for f in range(NF):
        kb = [(0, 1), (2, 3)][f % 2]
        kbanks = [2 * kb[0], 2 * kb[0] + 1, 2 * kb[1], 2 * kb[1] + 1][:max(NT, 1)]
        pe_claim(kbanks)
        vpe = 0
        for t in range(NT):
            out_ap = pb[kb[t // 2]][:, (t % 2) * TS:(t % 2) * TS + TS]
            p.op("tensor", lambda e, ff=f, o=out_ap: e.matmul(
                o, bo_row_bf[0:1, ff * 128:(ff + 1) * 128], ones_row_bf[0:1, 0:TS],
                start=True, stop=False), "pe")
            for kc in range(8):
                vpe = p.op("tensor", lambda e, k=kc, ff=f, tt=t, o=out_ap: e.matmul(
                    o, w_chunks(wo_bf, 8)[:, k, ff * 128:(ff + 1) * 128],
                    o_bf[k][:, tt * TS:(tt + 1) * TS], start=False, stop=(k == 7)), "pe")
        gp(lambda e, ff=f: e.tensor_sub(ybf[ff], Ax[:, ff, :], scr(0)))
        gp(lambda e, ff=f: e.tensor_mul(ybf[ff], ybf[ff], scr(1)))
        vrb = gp(lambda e, ff=f: e.tensor_scalar(
            ybf[ff], ybf[ff], CA1[:, ff:ff + 1], SA[:, ff:ff + 1], ALU.mult, ALU.add))
        p.wait("vector", "pe", vpe)
        p.wait("vector", "gpc", vrb)
        p.wait("vector", "dve", p.sems["dve"])
        vy = 0
        for hh in range(NH2):
            vy = p.op("vector", lambda e, ff=f, h=hh, b0=kb: e.tensor_add(
                ybf[ff][:, h * W1:h * W1 + W1],
                ybf[ff][:, h * W1:h * W1 + W1], pb[b0[h]][:, 0:W1]), "dve")
        for b in kbanks:
            bank_free[b] = ("dve", vy)
    v_y = p.sems["dve"]

    p.wait("scalar", "dve", v_y)
    ray, rby, vrow = ln_stats(ybf, sq_bf, persist=False)
    expand_rows(ray, rby, vrow)
    for f in range(NF):
        dve(lambda e, ff=f: e.tensor_sub(scr(2), ybf[ff], scr(0)))
        dve(lambda e: e.tensor_mul(scr(3), scr(2), scr(1)))
        dve(lambda e, ff=f: e.tensor_scalar(
            scr(2), scr(3), GAG[:, ff:ff + 1], GAB[:, ff:ff + 1], ALU.mult, ALU.add))
        dve(lambda e, ff=f: e.tensor_add(Ax[:, ff, :], Ax[:, ff, :], scr(2)))
    v_xa = p.sems["dve"]

    x2bf = [Ay_bf[:, f * T:(f + 1) * T] for f in range(NF)]
    p.wait("vector", "dve", v_xa)
    p.wait("vector", "pe", p.sems["pe"])
    p.wait("vector", "act", p.sems["act"])
    for f in range(NF):
        p.op("vector", lambda e, ff=f: e.tensor_copy(x2bf[ff], Ax[:, ff, :]), "dve")
    v_x2bf = p.sems["dve"]
    p.wait("scalar", "dve", v_x2bf)
    ra2, rb2, vrow = ln_stats(x2bf, sq_bf, persist=False)
    expand_rows(ra2, rb2, vrow)
    h2 = [Abf[:, f, :] for f in range(NF)]
    p.wait("vector", "pe", p.sems["pe"])
    pe_claim(list(range(NT)))
    p.wait("tensor", "dp", v_params)
    vg = 0
    for f in range(NF):
        dve(lambda e, ff=f: e.tensor_sub(scr(2), Ax[:, ff, :], scr(0)))
        dve(lambda e: e.tensor_mul(scr(3), scr(2), scr(1)))
        dve(lambda e, ff=f: e.tensor_scalar(
            scr(2), scr(3), CB1[:, ff:ff + 1], SB_[:, ff:ff + 1], ALU.mult, ALU.add))
        vh2 = dve(lambda e, ff=f: e.tensor_scalar(
            h2[ff], scr(3), CB1[:, ff:ff + 1], SB_[:, ff:ff + 1], ALU.mult, ALU.add))
        p.wait("tensor", "dve", vh2)
        for t in range(NT):
            vg = p.op("tensor", lambda e, ff=f, tt=t: e.matmul(
                bank(tt)[0:E, 0:TS], wg_sb[:, ff * E:(ff + 1) * E],
                scr(2)[:, tt * TS:(tt + 1) * TS], start=(ff == 0), stop=(ff == NF - 1)), "pe")
        if f < NF - 1:
            p.wait("vector", "pe", vg)
    g8 = scr(2)[0:8, :]
    p.wait("vector", "pe", p.sems["pe"])
    p.wait("vector", "act", p.sems["act"])
    p.wait("vector", "dve", p.sems["dve"])
    for t in range(NT):
        vge = p.op("vector", lambda e, tt=t: e.tensor_copy(
            g8[:, tt * TS:(tt + 1) * TS], bank(tt)[0:E, 0:TS]), "dve")
        bank_free[t] = ("dve", vge)
    tb = pb[3]
    pe_claim([6, 7])
    p.wait("tensor", "dve", max(p.sems["dve"], v_id))
    vtr = 0
    for c in range(TC):
        vtr = p.op("tensor", lambda e, cc_=c: e.transpose(
            tb[:, cc_ * E:(cc_ + 1) * E], g8[:, cc_ * 128:(cc_ + 1) * 128], ident[0:E, 0:E]), "pe")
    p.wait("vector", "pe", vtr)
    p.wait("vector", "dve", p.sems["dve"])
    p.op("vector", lambda e: e.tensor_copy(
        gtm[:].rearrange("p a b -> p (a b)"), tb[:, 0:TC * E]), "dve")
    bank_free[6] = ("dve", p.sems["dve"])
    bank_free[7] = ("dve", p.sems["dve"])

    dve(lambda e: e.reduce_max(st1[:, :, 0:1], gtm[:], axis=mybir.AxisListType.X))
    for c in range(TC):
        dve(lambda e, cc_=c: e.tensor_scalar(
            st2[:, cc_, :], gtm[:, cc_, :], st1[:, cc_, 0:1], None, ALU.subtract))
    p.wait("scalar", "dve", p.sems["dve"])
    p.wait("scalar", "act", p.sems["act"])
    vex = p.op("scalar", lambda e: e.activation(st2[:], st2[:], AF.Exp), "act")
    p.wait("vector", "act", vex)
    dve(lambda e: e.reduce_sum(st1[:, :, 0:1], st2[:], axis=mybir.AxisListType.X))
    dve(lambda e: e.reciprocal(st1[:, :, 1:2], st1[:, :, 0:1]))
    for c in range(TC):
        dve(lambda e, cc_=c: e.tensor_scalar(
            gtm[:, cc_, :], st2[:, cc_, :], st1[:, cc_, 1:2], None, ALU.mult))
    dve(lambda e: e.reduce_max(st1[:, :, 0:1], gtm[:], axis=mybir.AxisListType.X))
    for c in range(TC):
        dve(lambda e, cc_=c: e.tensor_scalar(
            st2[:, cc_, :], gtm[:, cc_, :], st1[:, cc_, 0:1], None, ALU.is_ge))
    dve(lambda e: e.scalar_tensor_tensor(st2[:], st2[:], -1e30, gtm[:], ALU.mult, ALU.add))
    dve(lambda e: e.reduce_max(st1[:, :, 2:3], st2[:], axis=mybir.AxisListType.X))
    for c in range(TC):
        dve(lambda e, cc_=c: e.tensor_scalar(
            st2[:, cc_, :], gtm[:, cc_, :], st1[:, cc_, 2:3], None, ALU.is_ge))
    dve(lambda e: e.tensor_mul(wtm[:], gtm[:], st2[:]))
    dve(lambda e: e.tensor_add(st1[:, :, 3:4], st1[:, :, 0:1], st1[:, :, 2:3]))
    dve(lambda e: e.tensor_scalar_add(st1[:, :, 3:4], st1[:, :, 3:4], 1e-8))
    dve(lambda e: e.reciprocal(st1[:, :, 4:5], st1[:, :, 3:4]))
    for c in range(TC):
        dve(lambda e, cc_=c: e.tensor_scalar(
            wtm[:, cc_, :], wtm[:, cc_, :], st1[:, cc_, 4:5], None, ALU.mult))
    v_wtm = p.sems["dve"]
    wTs = sq_bf_t[0:8, :]
    pe_claim([6, 7])
    p.wait("tensor", "dve", v_wtm)
    bs_tr = min(8, TC)
    for half in range(TC // bs_tr):
        vtr = 0
        for c8 in range(bs_tr):
            cc = half * bs_tr + c8
            vtr = p.op("tensor", lambda e, cc_=cc, j=c8: e.transpose(
                tb[0:E, j * 128:(j + 1) * 128], wtm[:, cc_, :], ident), "pe")
        p.wait("vector", "pe", vtr)
        p.wait("vector", "dve", p.sems["dve"])
        p.op("vector", lambda e, h=half, w_=bs_tr * 128: e.tensor_copy(
            wTs[:, h * w_:(h + 1) * w_], tb[0:E, 0:w_]), "dve")
        if half + 1 < TC // bs_tr:
            p.wait("tensor", "dve", p.sems["dve"])
    bank_free[6] = ("dve", p.sems["dve"])
    bank_free[7] = ("dve", p.sems["dve"])
    v_wT_dve = p.sems["dve"]

    slots = [Aw[:, i * 2048:(i + 1) * 2048] for i in range(6)]
    slot_state = [None] * 6
    slot_sems = [f"r{i}" for i in range(6)]
    slot_loads = [0] * 6
    p.wait("gpsimd", "pe", p.sems["pe"])

    def load_slab(slot, in_ap, a):
        if slot_state[slot] is not None:
            p.wait("gpsimd", *slot_state[slot])
            slot_state[slot] = None
        slot_loads[slot] += 1
        return p.dma("gpsimd", w_chunks(slots[slot], a), in_ap, slot_sems[slot])

    NB = max(1, T // 1024)
    NTH = max(1, W1 // TS)
    NU = 8 * NB
    wexp_b = Ay_bf[:, 0:T]
    t1_t = [Ay_bf[:, T + i * W1:T + (i + 1) * W1] for i in range(2)]
    s_bf = [Ay_bf[:, T + 4 * W1:T + 5 * W1] for i in range(1)]
    F_t = [Ascr_bf[:, i * T:(i + 1) * T] for i in range(8)]
    gv_slot_seq = [(2, 3), (4, 5), (0, 1), (2, 3)]
    w2_slots = [4, 5, 0, 1]
    wT_bf = sq_bf_t[0:8, :]

    def unit_ap(u, th=None):
        pt = pb[u % 4]
        if th is None:
            return pt[:, 0:W1]
        return pt[:, th * TS:th * TS + TS]

    def unit_banks(u):
        return [2 * (u % 4) + i for i in range(max(1, W1 // 512))]

    def emit_V(pend):
        ui, itl, tp, im, s12, s12_wait = pend
        pe_claim(unit_banks(ui))
        p.wait("tensor", slot_sems[s12], s12_wait)
        vv = 0
        for kc in range(8):
            for th in range(NTH):
                tt = tp * NTH + th
                vv = p.op("tensor", lambda e, k=kc, s_=s12, m=im, uu=ui, hh=th, t_=tt: e.matmul(
                    unit_ap(uu, hh), w_chunks(slots[s_], 8)[:, k, m * 128:(m + 1) * 128],
                    h2[k][:, t_ * TS:(t_ + 1) * TS],
                    start=(k == 0), stop=(k == 7)), "pe")
        if im == 1 and tp == NB - 1:
            slot_state[s12] = ("pe", vv)
        p.wait("vector", "pe", vv)
        p.wait("vector", "dve", p.sems["dve"])
        vF = p.op("vector", lambda e, uu=ui, ff=itl, t0=tp: e.tensor_mul(
            F_t[ff][:, t0 * W1:t0 * W1 + W1], t1_t[uu % 2][:, 0:W1], unit_ap(uu)), "dve")
        for b in unit_banks(ui):
            bank_free[b] = ("dve", vF)

    p.wait("tensor", "dve", v_wT_dve)
    v_wexp_used = 0
    for ei in range(E):
        wbanks = [4 + i for i in range(max(1, (2 * T) // 1024))][:4]
        pe_claim(wbanks)
        p.wait("tensor", "dve", v_wexp_used)
        p.wait("tensor", "dve", max(v_sel, v_wT_dve))
        vwe_pe = 0
        for t in range(NT):
            vwe_pe = p.op("tensor", lambda e, tt=t, ee=ei: e.matmul(
                pb[2 + tt // 2][:, (tt % 2) * TS:(tt % 2) * TS + TS] if T > 512 else pb[2][:, 0:TS],
                sel_bf[:, ee * 128:(ee + 1) * 128], wT_bf[0:8, tt * TS:(tt + 1) * TS],
                start=True, stop=True), "pe")
        p.wait("vector", "pe", vwe_pe)
        p.wait("vector", "dve", p.sems["dve"])
        vwe = 0
        for hh in range(NB):
            vwe = p.op("vector", lambda e, h=hh: e.tensor_copy(
                wexp_b[:, h * W1:h * W1 + W1],
                pb[2 + h][:, 0:W1] if T > 512 else pb[2][:, 0:W1]), "dve")
        for b in wbanks:
            bank_free[b] = ("dve", vwe)
        for ih in range(2):
            pend = None
            for it2 in range(4):
                s11, s12 = gv_slot_seq[it2]
                base_i = ih * 1024 + it2 * 256
                load_slab(s11, dW11[ei, :, base_i:base_i + 256].rearrange("(k p) c -> p k c", p=128), 8)
                load_slab(s12, dW12[ei, :, base_i:base_i + 256].rearrange("(k p) c -> p k c", p=128), 8)
                for im in range(2):
                    itl = it2 * 2 + im
                    for tp in range(NB):
                        ui = itl * NB + tp
                        pe_claim(unit_banks(ui))
                        p.wait("tensor", slot_sems[s11], 16 * slot_loads[s11])
                        vg = 0
                        for kc in range(8):
                            for th in range(NTH):
                                tt = tp * NTH + th
                                vg = p.op("tensor", lambda e, k=kc, s_=s11, m=im, uu=ui, hh=th, t_=tt: e.matmul(
                                    unit_ap(uu, hh), w_chunks(slots[s_], 8)[:, k, m * 128:(m + 1) * 128],
                                    h2[k][:, t_ * TS:(t_ + 1) * TS],
                                    start=(k == 0), stop=(k == 7)), "pe")
                        if im == 1 and tp == NB - 1:
                            slot_state[s11] = ("pe", vg)
                        p.wait("scalar", "pe", vg)
                        p.wait("scalar", "dve", p.sems["dve"])
                        p.wait("scalar", "act", p.sems["act"])
                        vs = p.op("scalar", lambda e, uu=ui: e.activation(
                            s_bf[0][:, 0:W1], unit_ap(uu), AF.Sigmoid), "act")
                        p.wait("vector", "act", vs)
                        p.wait("vector", "dve", p.sems["dve"])
                        vt1a = p.op("vector", lambda e, uu=ui: e.tensor_mul(
                            t1_t[uu % 2][:, 0:W1], unit_ap(uu), s_bf[0][:, 0:W1]), "dve")
                        p.wait("vector", "dve", vt1a)
                        vt1b = p.op("vector", lambda e, uu=ui, t0=tp: e.tensor_mul(
                            t1_t[uu % 2][:, 0:W1], t1_t[uu % 2][:, 0:W1],
                            wexp_b[:, t0 * W1:t0 * W1 + W1]), "dve")
                        for b in unit_banks(ui):
                            bank_free[b] = ("dve", vt1a)
                        if pend is not None:
                            emit_V(pend)
                        pend = (ui, itl, tp, im, s12, 16 * slot_loads[s12])
            emit_V(pend)
            pend = None
            v_F = p.sems["dve"]
            for ic2 in range(4):
                s2 = w2_slots[ic2]
                base_ic = ih * 1024 + ic2 * 256
                load_slab(s2, dW2[ei, base_ic:base_ic + 256, :].rearrange("(a p) c -> p a c", p=128), 2)
            for s2 in w2_slots:
                p.wait("tensor", slot_sems[s2], 16 * slot_loads[s2])
            p.wait("tensor", "dve", v_F)
            last_pe = 0
            for f in range(NF):
                obs = [4 * (f % 2) + t for t in range(NT)]
                pe_claim(obs)
                vpe = 0
                for ic in range(8):
                    for t in range(NT):
                        vpe = p.op("tensor", lambda e, i0=ic, ff=f, tt=t, b=obs[t]: e.matmul(
                            bank(b)[:, 0:TS],
                            w_chunks(slots[w2_slots[i0 // 2]], 2)[:, i0 % 2, ff * 128:(ff + 1) * 128],
                            F_t[i0][:, tt * TS:(tt + 1) * TS],
                            start=(i0 == 0), stop=(i0 == 7)), "pe")
                p.wait("vector", "pe", vpe)
                p.wait("vector", "dve", p.sems["dve"])
                vev = 0
                for t in range(NT):
                    vev = p.op("vector", lambda e, ff=f, tt=t, b=obs[t]: e.scalar_tensor_tensor(
                        Ax[:, ff, tt * TS:(tt + 1) * TS], bank(b)[:, 0:TS],
                        GB[:, ff:ff + 1], Ax[:, ff, tt * TS:(tt + 1) * TS],
                        ALU.mult, ALU.add), "dve")
                for b in obs:
                    bank_free[b] = ("dve", vev)
                last_pe = vpe
            for ic2 in range(4):
                slot_state[w2_slots[ic2]] = ("pe", last_pe)
        v_wexp_used = p.sems["dve"]

    v_out = p.sems["dve"]
    p.wait("sync", "dve", v_out)
    for f in range(NF):
        p.dma("sync", dy[f * 128:(f + 1) * 128, :], Ax[:, f, :], "dst")
    p.wait("sync", "dst", p.sems["dst"])

    for name in p.sems:
        p.sem_handles[name] = es.enter_context(nc.semaphore(name))
    with es:
        block = es.enter_context(nc.Block())

        @block.sync
        def _(e):
            for fn in p.thunks["sync"]:
                fn(e)

        @block.gpsimd
        def _(e):
            for fn in p.thunks["gpsimd"]:
                fn(e)

        @block.tensor
        def _(e):
            for fn in p.thunks["tensor"]:
                fn(e)

        @block.vector
        def _(e):
            for fn in p.thunks["vector"]:
                fn(e)

        @block.scalar
        def _(e):
            for fn in p.thunks["scalar"]:
                fn(e)

    return nc


_cache = {}


def kernel(L0_attn_x, L0_attn_Wq, L0_attn_Wk, L0_attn_Wv, L0_attn_Wbeta,
           L0_attn_b_beta, L0_attn_Wo, L0_attn_b_o, L0_attn_ln_g, L0_attn_ln_b,
           L0_ffn_Wg, L0_ffn_W11, L0_ffn_W12, L0_ffn_W2, L0_c, L0_wmod):
    x = np.asarray(L0_attn_x, dtype=np.float32)
    B, S, _ = x.shape
    assert B == 1 and S % NCORES == 0
    T = S // NCORES
    if "nc" not in _cache or _cache.get("T") != T:
        _cache["nc"] = build_program(T=T, TS=min(512, T))
        _cache["T"] = T
    nc = _cache["nc"]

    Wq_full = np.ascontiguousarray(L0_attn_Wq, np.float32)
    Wv_full = np.ascontiguousarray(L0_attn_Wv, np.float32)
    wmod_full = np.ascontiguousarray(L0_wmod, np.float32)
    shared = {
        "Wk": np.ascontiguousarray(L0_attn_Wk, np.float32),
        "Wbeta": np.ascontiguousarray(L0_attn_Wbeta, np.float32),
        "b_beta": np.ascontiguousarray(L0_attn_b_beta, np.float32),
        "Wo": np.ascontiguousarray(L0_attn_Wo, np.float32),
        "b_o": np.ascontiguousarray(L0_attn_b_o, np.float32),
        "ln_g": np.ascontiguousarray(L0_attn_ln_g, np.float32),
        "ln_b": np.ascontiguousarray(L0_attn_ln_b, np.float32),
        "Wg": np.ascontiguousarray(L0_ffn_Wg, np.float32),
        "W11": np.ascontiguousarray(L0_ffn_W11, np.float32),
        "W12": np.ascontiguousarray(L0_ffn_W12, np.float32),
        "W2": np.ascontiguousarray(L0_ffn_W2, np.float32),
        "cvec": np.ascontiguousarray(L0_c, np.float32),
    }
    MSH = (6 * D) // NCORES
    in_maps = []
    for cid in range(NCORES):
        xT = np.ascontiguousarray(x[0, cid * T:(cid + 1) * T, :].T)
        import ml_dtypes
        pmask = np.zeros((128, NCORES * NF), ml_dtypes.bfloat16)
        for j in range(NCORES):
            if j < cid:
                pmask[:, j * NF:(j + 1) * NF] = 1.0
        in_maps.append({**shared, "xT": xT, "pmask": pmask,
                        "Wq": np.ascontiguousarray(Wq_full[cid * 128:(cid + 1) * 128, :]),
                        "Wv": np.ascontiguousarray(Wv_full[cid * 128:(cid + 1) * 128, :]),
                        "wmod": np.ascontiguousarray(wmod_full[:, cid * MSH:(cid + 1) * MSH])})

    res = run_bass_kernel_spmd(nc, in_maps, core_ids=list(range(NCORES)))
    out = np.empty((1, S, D), np.float32)
    for cid in range(NCORES):
        out[0, cid * T:(cid + 1) * T, :] = res.results[cid]["yT"].T
    return out


# version 23
# speedup vs baseline: 1.0225x; 1.0129x over previous
import numpy as np
from contextlib import ExitStack

import concourse.bass as bass
import concourse.mybir as mybir
from concourse.bass_utils import run_bass_kernel_spmd

F32 = mybir.dt.float32
F32R = mybir.dt.float32r
BF16 = mybir.dt.bfloat16
AF = mybir.ActivationFunctionType
ALU = mybir.AluOpType

NCORES = 8
D = 1024
NH = 16
HD = 64
E = 8
II = 2048
NF = D // 128
EPS = 1e-5


class Prog:
    ENGS = ("sync", "gpsimd", "tensor", "vector", "scalar")

    def __init__(self):
        self.thunks = {e: [] for e in self.ENGS}
        self.sems = {}
        self.sem_handles = {}
        self.waited = {e: {} for e in self.ENGS}

    def declare(self, *names):
        for n in names:
            self.sems[n] = 0

    def wait(self, eng, sem, val):
        assert val <= self.sems[sem], f"wait {sem}>={val} > emitted {self.sems[sem]}"
        if val <= self.waited[eng].get(sem, 0):
            return
        self.waited[eng][sem] = val
        self.thunks[eng].append(lambda e, s=sem, v=val: e.wait_ge(self.sem_handles[s], v))

    def op(self, eng, build, inc=None, amt=1):
        if inc is None:
            self.thunks[eng].append(build)
            return None
        self.sems[inc] += amt
        self.thunks[eng].append(
            lambda e, b=build, s=inc: b(e).then_inc(self.sem_handles[s], amt))
        return self.sems[inc]

    def dma(self, eng, out, in_, inc, slow=False):
        if slow:
            return self.op(eng, lambda e: e.dma_start(
                out=out, in_=in_, allow_slow_non_contiguous=True), inc, 16)
        return self.op(eng, lambda e: e.dma_start(out=out, in_=in_), inc, 16)


def build_program(T=2048, TS=512):
    NT = T // TS
    TC = T // 128
    NH2 = max(1, NT // 2)
    W1 = min(1024, T)
    nc = bass.Bass("TRN2", target_bir_lowering=False)
    p = Prog()
    p.declare("dp", "dxx", "dxg", "dst", "swk", "swo", "sbo", "sci", "scg", "swb", "sr",
              "pe", "dve", "act", "gpc", "cc",
              *[f"r{i}" for i in range(6)], *[f"m{i}" for i in range(4)],
              "q0", "q1", "e0", "e1", "e2", "e3", "e4", "smod", "sg2")

    dx = nc.dram_tensor("xT", [D, T], F32, kind="ExternalInput").ap()
    dWq = nc.dram_tensor("Wq", [128, D], F32, kind="ExternalInput").ap()
    dWk = nc.dram_tensor("Wk", [D, D], F32, kind="ExternalInput").ap()
    dWv = nc.dram_tensor("Wv", [128, D], F32, kind="ExternalInput").ap()
    dWbeta = nc.dram_tensor("Wbeta", [D, NH], F32, kind="ExternalInput").ap()
    db_beta = nc.dram_tensor("b_beta", [NH], F32, kind="ExternalInput").ap()
    dWo = nc.dram_tensor("Wo", [D, D], F32, kind="ExternalInput").ap()
    db_o = nc.dram_tensor("b_o", [D], F32, kind="ExternalInput").ap()
    dln_g = nc.dram_tensor("ln_g", [D], F32, kind="ExternalInput").ap()
    dln_b = nc.dram_tensor("ln_b", [D], F32, kind="ExternalInput").ap()
    dWg = nc.dram_tensor("Wg", [D, E], F32, kind="ExternalInput").ap()
    dW11 = nc.dram_tensor("W11", [E, D, II], F32, kind="ExternalInput").ap()
    dW12 = nc.dram_tensor("W12", [E, D, II], F32, kind="ExternalInput").ap()
    dW2 = nc.dram_tensor("W2", [E, II, D], F32, kind="ExternalInput").ap()
    dc = nc.dram_tensor("cvec", [1, D], F32, kind="ExternalInput").ap()
    dwmod = nc.dram_tensor("wmod", [D, (6 * D) // NCORES], F32, kind="ExternalInput").ap()
    dpmask = nc.dram_tensor("pmask", [128, NCORES * NF], BF16, kind="ExternalInput").ap()
    dy = nc.dram_tensor("yT", [D, T], F32, kind="ExternalOutput").ap()
    dcc_in = nc.dram_tensor("cc_in", [128, 2 * NF], F32)
    dcc_out = nc.dram_tensor("cc_out", [NCORES * 128, 2 * NF], F32)

    es = ExitStack()
    sbuf = lambda n, sh, dt: es.enter_context(nc.sbuf_tensor(n, sh, dt))

    Ax = sbuf("Ax", [128, NF, T], F32)
    Ascr = sbuf("Ascr", [128, 4 * T], F32)
    Abf = sbuf("Abf", [128, NF, T], BF16)
    Ay = sbuf("Ay", [128, 4 * T], F32)
    Aw = sbuf("Aw", [128, 6 * 2048], BF16)
    aux = sbuf("aux", [128, T], F32)
    comp = aux[0:48, :]
    rows = aux[:, :]
    sq_bf_t = sbuf("sqbf", [128, T], BF16)
    mod_cols = sbuf("mod_cols", [128, 10 * NF], F32)
    smalls = sbuf("smalls", [128, 448], F32)
    smallsb = sbuf("smallsb", [128, 32 + TS + D + 2048], BF16)
    wcomb = sbuf("wcomb", [128, NF * 80], BF16)
    wg_sb = sbuf("wg_sb", [128, NF * E], F32)
    gathsb = sbuf("gathsb", [128, NCORES, 2 * NF], F32)
    summ = sbuf("summ", [128, 2 * NF], F32)
    minit = sbuf("minit", [128, 2 * NF], F32)
    pmask_sb = sbuf("pmask_sb", [128, NCORES * NF], BF16)

    Ascr_bf = Ascr[:].bitcast(BF16)
    gtm = Ay[:, 0:TC * E].rearrange("p (a b) -> p a b", b=E)
    wtm = Ay[:, TC * E:2 * TC * E].rearrange("p (a b) -> p a b", b=E)
    st2 = Ay[:, 2 * TC * E:3 * TC * E].rearrange("p (a b) -> p a b", b=E)
    st1 = Ay[:, 3 * TC * E:3 * TC * E + 5 * TC].rearrange("p (a b) -> p a b", b=5)
    Ay_bf = Ay[:].bitcast(BF16)
    Aw_f32 = Aw[:].bitcast(F32)

    pb = [es.enter_context(nc.psum_tensor(f"pb{i}", [128, 1024], F32)) for i in range(4)]

    def bank(i):
        return pb[i // 2][:, (i % 2) * 512:(i % 2) * 512 + 512]

    pr_rows = Aw[0:2, 8192:8192 + T]
    bank_free = {}

    def pe_claim(banks):
        for b in banks:
            st = bank_free.pop(b, None)
            if st is not None:
                if isinstance(st, list):
                    for sem, val in st:
                        p.wait("tensor", sem, val)
                else:
                    p.wait("tensor", st[0], st[1])

    def scr(i):
        return Ascr[:, i * T:(i + 1) * T]

    def w_chunks(view, nk):
        return view.rearrange("p (k c) -> p k c", k=nk)

    c_col = smalls[:, 0:NF]
    p.dma("sync", c_col, dc.rearrange("1 (a b) -> b a", b=128), "dp", slow=True)
    lng_col = smalls[:, 16:16 + NF]
    lnb_col = smalls[:, 32:32 + NF]
    p.dma("sync", lng_col, dln_g.rearrange("(a b) -> b a", b=128), "dp", slow=True)
    p.dma("sync", lnb_col, dln_b.rearrange("(a b) -> b a", b=128), "dp", slow=True)
    bb16 = smalls[64:80, 48:49]
    p.dma("sync", bb16, db_beta.unsqueeze(1), "dp")
    v_wcz = p.op("vector", lambda e: e.memset(wcomb[:], 0.0), "dve")
    p.wait("gpsimd", "dve", v_wcz)
    p.dma("gpsimd", wcomb[:].rearrange("p (f h) -> p f h", h=80)[:, :, 64:80],
          dWbeta.rearrange("(f p) h -> p f h", p=128), "swb")
    p.dma("sync", wg_sb[:].rearrange("p (f e) -> p f e", e=E),
          dWg.rearrange("(f p) e -> p f e", p=128), "dp")
    p.dma("sync", pmask_sb[:], dpmask, "dp")
    v_params = p.sems["dp"]

    identr = smalls[:, 64:192]
    v_iota = p.op("gpsimd", lambda e: e.iota(identr, pattern=[[-1, 128]], base=0,
                                             channel_multiplier=1,
                                             allow_small_or_imprecise_dtypes=True), "gpc")
    ident = smalls[:, 192:320]
    p.wait("vector", "gpc", v_iota)
    p.op("vector", lambda e: e.tensor_single_scalar(ident, identr, 0.0, ALU.is_equal), "dve")
    v_id = p.sems["dve"]
    ones_col_bf = smallsb[:, 0:1]
    p.op("vector", lambda e: e.memset(ones_col_bf, 1.0), "dve")
    ones_row_bf = smallsb[0:1, 32:32 + TS]
    p.op("vector", lambda e: e.memset(ones_row_bf, 1.0), "dve")
    v_ones = p.sems["dve"]
    bo_row_bf = smallsb[0:1, 32 + TS:32 + TS + D]
    sel_bf = smallsb[0:8, 32 + TS + D:32 + TS + D + 1024]
    v_selr = p.op("gpsimd", lambda e: e.iota(sel_bf, pattern=[[1, 8], [0, 128]], base=0,
                                             channel_multiplier=-1,
                                             allow_small_or_imprecise_dtypes=True), "gpc")
    p.wait("vector", "gpc", v_selr)
    p.op("vector", lambda e: e.tensor_single_scalar(sel_bf, sel_bf, 0.0, ALU.is_equal), "dve")
    v_sel = p.sems["dve"]
    E3 = smallsb[:, 32 + TS + D + 1024:32 + TS + D + 2048]
    for b0 in (0, 32, 64):
        p.op("gpsimd", lambda e, bb=b0: e.iota(E3[bb:bb + 16, :], pattern=[[1, 16], [0, 64]],
                                               base=0, channel_multiplier=-1,
                                               allow_small_or_imprecise_dtypes=True), "gpc")
    v_e3r = p.sems["gpc"]
    p.wait("vector", "gpc", v_e3r)
    for b0 in (0, 32, 64):
        p.op("vector", lambda e, bb=b0: e.tensor_single_scalar(
            E3[bb:bb + 16, :], E3[bb:bb + 16, :], 0.0, ALU.is_equal), "dve")
    v_e3 = p.sems["dve"]
    v_sel2 = v_sel

    eps_col = smalls[:, 424:425]
    p.op("vector", lambda e: e.memset(eps_col, EPS), "dve")
    v_eps = p.sems["dve"]
    silu_c = smalls[:, 320:320 + NF]
    p.wait("scalar", "dp", v_params)
    v_siluc = p.op("scalar", lambda e: e.activation(silu_c, c_col, AF.Silu), "act")
    v_bob = p.dma("gpsimd", bo_row_bf, db_o.unsqueeze(0), "sbo")
    vx = []
    xsem = []
    for f in range(NF):
        sem = "dxx" if f % 2 == 0 else "dxg"
        xsem.append(sem)
        if f >= 2:
            p.wait("scalar", sem, vx[f - 2])
        vx.append(p.dma("scalar", Ax[:, f, :], dx[f * 128:(f + 1) * 128, :], sem))

    MSH = (6 * D) // NCORES
    dcc2_in = nc.dram_tensor("cc2_in", [MSH + 2 * 128 * NH], F32)
    dcc2_out = nc.dram_tensor("cc2_out", [NCORES, MSH + 2 * 128 * NH], F32)
    wm_view = [Aw_f32[:, i * MSH:(i + 1) * MSH] for i in range(4)]
    wm_state = [None] * 4
    p.wait("tensor", "act", v_siluc)
    pe_claim([4, 5])
    vmm = 0
    for kc in range(8):
        slot = kc % 4
        if wm_state[slot] is not None:
            p.wait("gpsimd", *wm_state[slot])
            wm_state[slot] = None
        vld = p.dma("gpsimd", wm_view[slot], dwmod[kc * 128:(kc + 1) * 128, :], f"m{slot}")
        p.wait("tensor", f"m{slot}", vld)
        for j2, (o0, w0) in enumerate(((0, 512), (512, MSH - 512))):
            vmm = p.op("tensor", lambda e, s_=slot, k=kc, o=o0, w=w0, j=j2: e.matmul(
                bank(4 + j)[0:1, 0:w], silu_c[:, k:k + 1],
                wm_view[s_][:, o:o + w], start=(k == 0), stop=(k == 7)), "pe")
        wm_state[slot] = ("pe", p.sems["pe"])
    v_modpe = p.sems["pe"]
    p.wait("vector", "pe", vmm)
    p.wait("vector", "dve", p.sems["dve"])
    p.op("vector", lambda e: e.tensor_copy(Ascr[0:1, 0:512], bank(4)[0:1, 0:512]), "dve")
    vev = p.op("vector", lambda e: e.tensor_copy(
        Ascr[32:33, 0:MSH - 512], bank(5)[0:1, 0:MSH - 512]), "dve")
    bank_free[4] = ("dve", vev)
    bank_free[5] = ("dve", vev)
    p.wait("sync", "dve", vev)
    p.dma("sync", dcc2_in.ap()[0:512].unsqueeze(0), Ascr[0:1, 0:512], "smod")
    p.dma("sync", dcc2_in.ap()[512:MSH].unsqueeze(0), Ascr[32:33, 0:MSH - 512], "smod")

    rq_view = [Aw_f32[:, 4 * MSH + i * D:4 * MSH + (i + 1) * D] for i in range(2)]
    rq_stage = scr(3)[:, 0:64]
    for i, dmat in enumerate((dWv, dWq)):
        vld = p.dma("gpsimd", rq_view[i], dmat, f"q{i}")
        p.wait("vector", f"q{i}", vld)
        p.wait("vector", "dve", p.sems["dve"])
        for h in range(NH):
            p.op("vector", lambda e, si_=i, hh=h: e.reduce_sum(
                rq_stage[:, si_ * NH + hh:si_ * NH + hh + 1],
                rq_view[si_][:, hh * HD:(hh + 1) * HD], axis=mybir.AxisListType.X), "dve")
    vrq = p.sems["dve"]
    p.wait("sync", "dve", vrq)
    p.dma("sync", dcc2_in.ap()[MSH:MSH + 2048].rearrange("(pp h) -> pp h", pp=128),
          rq_stage[:, 0:NH], "smod")
    v_in2 = p.dma("sync", dcc2_in.ap()[MSH + 2048:MSH + 4096].rearrange(
        "(pp h) -> pp h", pp=128), rq_stage[:, NH:2 * NH], "smod")
    p.wait("gpsimd", "smod", v_in2)
    v_cc2 = p.op("gpsimd", lambda e: e.collective_compute(
        "AllGather", ALU.bypass, replica_groups=[list(range(NCORES))],
        ins=[dcc2_in.ap()], outs=[dcc2_out.ap()]), "cc")
    p.wait("sync", "cc", v_cc2)
    MQ = MSH // 128
    for j in range(NCORES):
        v_mcols = p.dma("sync", mod_cols[:, j * MQ:(j + 1) * MQ],
                        dcc2_out.ap()[j:j + 1, 0:MSH].rearrange(
                            "a (q pp) -> (a pp) q", pp=128), "smod", slow=True)
    p.wait("gpsimd", "cc", v_cc2)
    p.dma("gpsimd", wcomb[:].rearrange("p (k s) -> p k s", s=80)[:, :, 0:NH],
          dcc2_out.ap()[:, MSH:MSH + 2048].rearrange("j (pp h) -> pp j h", pp=128), "sg2")
    v_wq1g = p.dma("gpsimd", wcomb[:].rearrange("p (k s) -> p k s", s=80)[:, :, 32:32 + NH],
                   dcc2_out.ap()[:, MSH + 2048:MSH + 4096].rearrange(
                       "j (pp h) -> pp j h", pp=128), "sg2")
    p.wait("vector", "smod", p.sems["smod"])
    p.wait("scalar", "smod", p.sems["smod"])
    p.wait("vector", "dve", p.sems["dve"])
    SA, CA1, GA, SB_, CB1, GB, GAG, GAB, SC1, SC2 = [
        mod_cols[:, i * NF:(i + 1) * NF] for i in range(10)]

    def dve(build):
        p.wait("vector", "dve", p.sems["dve"])
        return p.op("vector", build, "dve")

    def clamp_gate(g):
        dve(lambda e: e.tensor_single_scalar(SC1, g, 0.0, ALU.is_ge))
        dve(lambda e: e.tensor_scalar(SC1, SC1, 2.0, -1.0, ALU.mult, ALU.add))
        dve(lambda e: e.tensor_mul(SC2, g, SC1))
        dve(lambda e: e.tensor_scalar_max(SC2, SC2, 0.25))
        dve(lambda e: e.tensor_mul(g, SC2, SC1))

    clamp_gate(GA)
    clamp_gate(GB)
    dve(lambda e: e.tensor_scalar_add(CA1, CA1, 1.0))
    dve(lambda e: e.tensor_scalar_add(CB1, CB1, 1.0))
    p.wait("vector", "dp", v_params)
    dve(lambda e: e.tensor_mul(GAG, GA, lng_col))
    dve(lambda e: e.tensor_mul(GAB, GA, lnb_col))
    v_mod_done = p.sems["dve"]
    v_wqv1 = vrq

    def ln_stats(src, sq_bf, persist=False, pre=None, vpe_guard=None):
        vg = p.sems["pe"] if vpe_guard is None else vpe_guard
        if vg:
            p.wait("scalar", "pe", vg)
        sq2 = Ascr_bf[:, 6 * T:7 * T]
        pe_claim(list(range(2 * NT)))
        vpe_f = []
        for f in range(NF):
            if pre is not None:
                pre(f)
            p.wait("scalar", "act", p.sems["act"])
            if f >= 2:
                p.wait("scalar", "pe", vpe_f[f - 2])
            sq = sq_bf if f % 2 == 0 else sq2
            vsq = p.op("scalar", lambda e, ff=f, q=sq: e.activation(q, src[ff], AF.Square), "act")
            p.wait("tensor", "act", vsq)
            for t in range(NT):
                p.op("tensor", lambda e, ff=f, tt=t: e.matmul(
                    bank(tt)[0:1, 0:TS], ones_col_bf,
                    src[ff][:, tt * TS:(tt + 1) * TS],
                    start=(ff == 0), stop=(ff == NF - 1)), "pe")
                p.op("tensor", lambda e, ff=f, tt=t, q=sq: e.matmul(
                    bank(NT + tt)[0:1, 0:TS], ones_col_bf,
                    q[:, tt * TS:(tt + 1) * TS],
                    start=(ff == 0), stop=(ff == NF - 1)), "pe")
            vpe_f.append(p.sems["pe"])
        v_st = p.sems["pe"]
        ra = scr(2)[0:1, :]
        rb = scr(3)[0:1, :]
        rc = scr(0)[0:1, :]
        p.wait("scalar", "pe", v_st)
        p.wait("scalar", "act", p.sems["act"])
        p.wait("scalar", "dve", p.sems["dve"])
        for t in range(NT):
            p.op("scalar", lambda e, tt=t: e.activation(
                ra[:, tt * TS:(tt + 1) * TS], bank(tt)[0:1, 0:TS], AF.Copy,
                scale=1.0 / D), "act")
            va = p.op("scalar", lambda e, tt=t: e.activation(
                rb[:, tt * TS:(tt + 1) * TS], bank(NT + tt)[0:1, 0:TS], AF.Copy,
                scale=1.0 / D), "act")
            bank_free[t] = ("act", va)
            bank_free[NT + t] = ("act", va)
        p.wait("scalar", "act", p.sems["act"])
        vmu = p.op("scalar", lambda e: e.activation(rc, ra, AF.Square), "act")
        p.wait("vector", "act", vmu)
        p.wait("vector", "dve", p.sems["dve"])
        vvar = p.op("vector", lambda e: e.tensor_sub(rb, rb, rc), "dve")
        p.wait("scalar", "dve", max(vvar, v_eps))
        p.wait("scalar", "act", p.sems["act"])
        vsq2 = p.op("scalar", lambda e: e.activation(
            rc, rb, AF.Sqrt, bias=eps_col[0:1, 0:1]), "act")
        p.wait("vector", "act", vsq2)
        p.wait("vector", "dve", p.sems["dve"])
        vrec = p.op("vector", lambda e: e.reciprocal(rb, rc), "dve")
        if persist:
            p.wait("gpsimd", "dve", vrec)
            p.wait("gpsimd", "pe", p.sems["pe"])
            p.dma("gpsimd", pr_rows[0:1, :], ra, "sr")
            p.dma("gpsimd", pr_rows[1:2, :], rb, "sr")
        return ra, rb, vrec

    NB2 = max(1, T // 1024)
    NTH2 = max(1, min(1024, T) // TS)
    WH = min(1024, T)

    ones_f32row = smalls[0:1, 64:192]
    p.wait("vector", "dve", v_id)
    p.op("vector", lambda e: e.memset(ones_f32row[:, 0:128], 1.0), "dve")
    v_onesf = p.sems["dve"]

    def expand_rows(mu_row, r_row, vrow_dve):
        pe_claim(list(range(8)))
        p.wait("tensor", "dve", max(vrow_dve, v_onesf))
        vmm = 0
        for t in range(NT):
            vmm = p.op("tensor", lambda e, tt=t: e.matmul(
                pb[tt // 2][:, (tt % 2) * TS:(tt % 2) * TS + TS],
                ones_f32row[:, 0:128], mu_row[:, tt * TS:(tt + 1) * TS],
                start=True, stop=True), "pe")
            vmm = p.op("tensor", lambda e, tt=t: e.matmul(
                pb[2 + tt // 2][:, (tt % 2) * TS:(tt % 2) * TS + TS],
                ones_f32row[:, 0:128], r_row[:, tt * TS:(tt + 1) * TS],
                start=True, stop=True), "pe")
        p.wait("vector", "pe", vmm)
        p.wait("vector", "dve", p.sems["dve"])
        vcp = 0
        for h in range(NB2):
            p.op("vector", lambda e, hh=h: e.tensor_copy(
                scr(0)[:, hh * WH:(hh + 1) * WH], pb[hh][:, 0:WH]), "dve")
            vcp = p.op("vector", lambda e, hh=h: e.tensor_copy(
                scr(1)[:, hh * WH:(hh + 1) * WH], pb[2 + hh][:, 0:WH]), "dve")
        for b in range(8):
            bank_free[b] = ("dve", vcp)
        return vcp

    def expand_rows2(rows2, vdep):
        p.wait("tensor", "sr", p.sems["sr"])
        pe_claim(list(range(8)))
        p.wait("tensor", "dve", max(vdep, v_sel2))
        vmm = 0
        for t in range(NT):
            vmm = p.op("tensor", lambda e, tt=t: e.matmul(
                pb[tt // 2][:, (tt % 2) * TS:(tt % 2) * TS + TS],
                sel_bf[0:2, 0:128], rows2[:, tt * TS:(tt + 1) * TS],
                start=True, stop=True), "pe")
            vmm = p.op("tensor", lambda e, tt=t: e.matmul(
                pb[2 + tt // 2][:, (tt % 2) * TS:(tt % 2) * TS + TS],
                sel_bf[0:2, 128:256], rows2[:, tt * TS:(tt + 1) * TS],
                start=True, stop=True), "pe")
        p.wait("vector", "pe", vmm)
        p.wait("vector", "dve", p.sems["dve"])
        vcp = 0
        for h in range(NB2):
            p.op("vector", lambda e, hh=h: e.tensor_copy(
                scr(0)[:, hh * WH:(hh + 1) * WH], pb[hh][:, 0:WH]), "dve")
            vcp = p.op("vector", lambda e, hh=h: e.tensor_copy(
                scr(1)[:, hh * WH:(hh + 1) * WH], pb[2 + hh][:, 0:WH]), "dve")
        for b in range(8):
            bank_free[b] = ("dve", vcp)
        return vcp

    xbf = [Ay_bf[:, f * T:(f + 1) * T] for f in range(NF)]

    def cast_x(f):
        p.wait("scalar", xsem[f], vx[f])
        p.wait("scalar", "act", p.sems["act"])
        p.op("scalar", lambda e, ff=f: e.copy(xbf[ff], Ax[:, ff, :]), "act")

    p.wait("vector", "dxx", vx[NF - 2])
    p.wait("vector", "dxg", vx[NF - 1])
    sq_bf = sq_bf_t[:, :]
    ra1, rb1, vrow = ln_stats(xbf, sq_bf, persist=True, pre=cast_x, vpe_guard=0)
    expand_rows(ra1, rb1, vrow)
    p.wait("vector", "sr", p.sems["sr"])
    p.wait("vector", "dve", v_mod_done)
    h1 = [Abf[:, f, :] for f in range(NF)]
    vh1a = [0] * NF
    for f in range(NF):
        buf = scr(2) if f % 2 == 0 else scr(3)
        if f >= 2:
            p.wait("vector", "act", vh1a[f - 2])
        dve(lambda e, ff=f, b=buf: e.tensor_sub(b, Ax[:, ff, :], scr(0)))
        v2 = dve(lambda e, b=buf: e.tensor_mul(b, b, scr(1)))
        p.wait("scalar", "dve", v2)
        p.wait("scalar", "act", p.sems["act"])
        vh1a[f] = p.op("scalar", lambda e, ff=f, b=buf: e.activation(
            h1[ff], b, AF.Identity, bias=SA[:, ff:ff + 1], scale=CA1[:, ff:ff + 1]), "act")
    v_h1 = p.sems["dve"]
    v_h1a = p.sems["act"]

    p.wait("tensor", "dve", max(v_h1, v_wqv1))
    p.wait("tensor", "act", v_h1a)
    p.wait("tensor", "dp", v_params)
    p.wait("tensor", "swb", 16)
    p.wait("tensor", "sg2", v_wq1g)
    for t in range(NT):
        bb = 2 * (t % 2)
        pe_claim([bb])
        vpe = 0
        for kc in range(8):
            vpe = p.op("tensor", lambda e, k=kc, b=bb, tt=t: e.matmul(
                bank(b)[0:80, 0:TS], wcomb[:, k * 80:(k + 1) * 80],
                h1[k][:, tt * TS:(tt + 1) * TS], start=(k == 0), stop=(k == 7)), "pe")
        p.wait("scalar", "pe", vpe)
        p.wait("scalar", "act", p.sems["act"])
        va = p.op("scalar", lambda e, tt=t, b=bb: e.activation(
            aux[64:80, tt * TS:(tt + 1) * TS], bank(b)[64:80, 0:TS],
            AF.Sigmoid, bias=bb16), "act")
        p.wait("vector", "pe", vpe)
        p.wait("vector", "dve", p.sems["dve"])
        vd = p.op("vector", lambda e, tt=t, b=bb: e.tensor_copy(
            aux[0:48, tt * TS:(tt + 1) * TS], bank(b)[0:48, 0:TS]), "dve")
        bank_free[bb] = [("dve", vd), ("act", va)]
    p.wait("vector", "act", p.sems["act"])
    p.wait("vector", "pe", p.sems["pe"])
    p.wait("vector", "dve", p.sems["dve"])
    p.op("vector", lambda e: e.tensor_copy(sq_bf_t[0:48, :], aux[0:48, :]), "dve")
    v_cbf = p.op("vector", lambda e: e.tensor_copy(sq_bf_t[64:80, :], aux[64:80, :]), "dve")

    p.wait("gpsimd", "pe", v_modpe)
    p.wait("gpsimd", "dve", vrq)
    wk_bf = Aw[:, 0:8 * D]
    v_wk = p.dma("gpsimd", w_chunks(wk_bf, 8), dWk.rearrange("(k p) c -> p k c", p=128), "swk")

    u = [Ay_bf[:, f * T:(f + 1) * T] for f in range(NF)]
    p.wait("tensor", "swk", 16)
    p.wait("tensor", "dve", v_cbf)
    NB512 = max(1, WH // 512)
    itA = 0
    for f in range(NF):
        for hh in range(NB2):
            pk = 0 if itA % 2 == 0 else 2
            pv = 1 if itA % 2 == 0 else 3
            vsl = itA % 2
            kb = [2 * pk, 2 * pk + 1][:NB512]
            vbk = [2 * pv, 2 * pv + 1][:NB512]
            pe_claim(vbk)
            vv = 0
            for th in range(NTH2):
                tt = hh * NTH2 + th
                vv = p.op("tensor", lambda e, ff=f, t_=tt, o=th, pv_=pv: e.matmul(
                    pb[pv_][:, o * TS:o * TS + TS], E3[0:16, ff * 128:(ff + 1) * 128],
                    sq_bf_t[0:16, t_ * TS:(t_ + 1) * TS], start=True, stop=True), "pe")
            pe_claim(kb)
            vk = 0
            for kc in range(8):
                for th in range(NTH2):
                    tt = hh * NTH2 + th
                    vk = p.op("tensor", lambda e, k=kc, ff=f, t_=tt, o=th, pk_=pk: e.matmul(
                        pb[pk_][:, o * TS:o * TS + TS],
                        w_chunks(wk_bf, 8)[:, k, ff * 128:(ff + 1) * 128],
                        h1[k][:, t_ * TS:(t_ + 1) * TS], start=(k == 0), stop=(k == 7)), "pe")
            p.wait("scalar", "pe", vv)
            p.wait("scalar", "act", p.sems["act"])
            p.wait("scalar", "dve", p.sems["dve"])
            vvs = p.op("scalar", lambda e, pv_=pv, s_=vsl: e.copy(
                scr(s_)[:, 0:WH], pb[pv_][:, 0:WH]), "act")
            for b in vbk:
                bank_free[b] = ("act", vvs)
            p.wait("vector", "pe", vk)
            p.wait("vector", "act", vvs)
            p.wait("vector", "dve", p.sems["dve"])
            vu = p.op("vector", lambda e, ff=f, h=hh, pk_=pk, s_=vsl: e.tensor_mul(
                u[ff][:, h * WH:h * WH + WH], pb[pk_][:, 0:WH], scr(s_)[:, 0:WH]), "dve")
            for b in kb:
                bank_free[b] = ("dve", vu)
            itA += 1
    o_bf = h1
    itB = 0
    for f in range(NF):
        for hh in range(NB2):
            pbe = 2 if itB % 2 == 0 else 0
            pq = 3 if itB % 2 == 0 else 1
            beb = [2 * pbe, 2 * pbe + 1][:NB512]
            qqb = [2 * pq, 2 * pq + 1][:NB512]
            pe_claim(beb)
            pe_claim(qqb)
            vb_ = vq = 0
            for th in range(NTH2):
                tt = hh * NTH2 + th
                vb_ = p.op("tensor", lambda e, ff=f, t_=tt, o=th, pp=pbe: e.matmul(
                    pb[pp][:, o * TS:o * TS + TS], E3[64:80, ff * 128:(ff + 1) * 128],
                    sq_bf_t[64:80, t_ * TS:(t_ + 1) * TS], start=True, stop=True), "pe")
                vq = p.op("tensor", lambda e, ff=f, t_=tt, o=th, pp=pq: e.matmul(
                    pb[pp][:, o * TS:o * TS + TS], E3[32:48, ff * 128:(ff + 1) * 128],
                    sq_bf_t[32:48, t_ * TS:(t_ + 1) * TS], start=True, stop=True), "pe")
            p.wait("vector", "pe", max(vb_, vq))
            p.wait("vector", "dve", p.sems["dve"])
            ini_m = 0.0 if hh == 0 else scr(2)[:, hh * WH - 1:hh * WH]
            p.op("vector", lambda e, ff=f, h=hh, im_=ini_m, pp=pbe: e.tensor_tensor_scan(
                scr(2)[:, h * WH:h * WH + WH], pb[pp][:, 0:WH], u[ff][:, h * WH:h * WH + WH],
                im_, ALU.mult, ALU.add), "dve")
            ini_b = 1.0 if hh == 0 else scr(3)[:, hh * WH - 1:hh * WH]
            p.wait("vector", "dve", p.sems["dve"])
            vsc = p.op("vector", lambda e, ff=f, h=hh, ib=ini_b, pp=pbe: e.tensor_tensor_scan(
                scr(3)[:, h * WH:h * WH + WH], pb[pp][:, 0:WH],
                u[ff][:, h * WH:h * WH + WH], ib, ALU.mult, ALU.bypass), "dve")
            for b in beb:
                bank_free[b] = ("dve", vsc)
            p.wait("vector", "dve", p.sems["dve"])
            p.op("vector", lambda e, ff=f, h=hh, pp=pq: e.tensor_mul(
                o_bf[ff][:, h * WH:h * WH + WH], scr(2)[:, h * WH:h * WH + WH],
                pb[pp][:, 0:WH]), "dve")
            p.wait("vector", "dve", p.sems["dve"])
            vqB = p.op("vector", lambda e, ff=f, h=hh, pp=pq: e.tensor_mul(
                u[ff][:, h * WH:h * WH + WH], scr(3)[:, h * WH:h * WH + WH],
                pb[pp][:, 0:WH]), "dve")
            for b in qqb:
                bank_free[b] = ("dve", vqB)
            itB += 1
        dve(lambda e, ff=f: e.tensor_copy(summ[:, ff:ff + 1], scr(2)[:, T - 1:T]))
        dve(lambda e, ff=f: e.tensor_copy(summ[:, NF + ff:NF + ff + 1], scr(3)[:, T - 1:T]))
    v_summ = p.sems["dve"]

    p.wait("gpsimd", "dve", v_summ)
    v_ci = p.dma("gpsimd", dcc_in.ap(), summ[:], "sci")
    p.wait("gpsimd", "sci", 16)
    v_cc = p.op("gpsimd", lambda e: e.collective_compute(
        "AllGather", ALU.bypass, replica_groups=[list(range(NCORES))],
        ins=[dcc_in.ap()], outs=[dcc_out.ap()]), "cc")
    p.wait("gpsimd", "cc", v_cc)
    v_cg = p.dma("gpsimd", gathsb[:], dcc_out.ap().rearrange("(j p) q -> p j q", p=128), "scg")
    mA, mB = minit[:, 0:NF], minit[:, NF:2 * NF]
    sA, sB = smalls[:, 336:336 + NF], smalls[:, 344:344 + NF]
    p.wait("vector", "scg", 16)
    p.wait("vector", "dp", v_params)
    dve(lambda e: e.memset(mA, 0.0))
    for j in range(NCORES - 1):
        pm = pmask_sb[:, j * NF:(j + 1) * NF]
        dve(lambda e, jj=j, m=pm: e.tensor_mul(sA, gathsb[:, jj, 0:NF], m))
        dve(lambda e, jj=j: e.tensor_scalar_add(sB, gathsb[:, jj, NF:2 * NF], -1.0))
        dve(lambda e, m=pm: e.tensor_mul(sB, sB, m))
        dve(lambda e: e.tensor_scalar_add(sB, sB, 1.0))
        dve(lambda e: e.tensor_mul(mB, mA, sB))
        dve(lambda e: e.tensor_add(mA, mB, sA))
    v_minit = p.sems["dve"]

    for f in range(NF):
        dve(lambda e, ff=f: e.scalar_tensor_tensor(
            o_bf[ff], u[ff], minit[:, ff:ff + 1], o_bf[ff], ALU.mult, ALU.add))
    v_o = p.sems["dve"]

    wo_bf = Aw[:, 0:8 * D]
    p.wait("gpsimd", "pe", p.sems["pe"])
    v_wo = p.dma("gpsimd", w_chunks(wo_bf, 8), dWo.rearrange("(k p) c -> p k c", p=128), "swo")
    expand_rows2(pr_rows, p.sems["dve"])
    ybf = [Ay_bf[:, f * T:(f + 1) * T] for f in range(NF)]
    p.wait("tensor", "dve", v_o)
    p.wait("tensor", "swo", 16)
    p.wait("tensor", "sbo", 16)
    for f in range(NF):
        kb = [(0, 1), (2, 3)][f % 2]
        kbanks = [2 * kb[0], 2 * kb[0] + 1, 2 * kb[1], 2 * kb[1] + 1][:max(NT, 1)]
        pe_claim(kbanks)
        vpe = 0
        for t in range(NT):
            out_ap = pb[kb[t // 2]][:, (t % 2) * TS:(t % 2) * TS + TS]
            p.op("tensor", lambda e, ff=f, o=out_ap: e.matmul(
                o, bo_row_bf[0:1, ff * 128:(ff + 1) * 128], ones_row_bf[0:1, 0:TS],
                start=True, stop=False), "pe")
            for kc in range(8):
                vpe = p.op("tensor", lambda e, k=kc, ff=f, tt=t, o=out_ap: e.matmul(
                    o, w_chunks(wo_bf, 8)[:, k, ff * 128:(ff + 1) * 128],
                    o_bf[k][:, tt * TS:(tt + 1) * TS], start=False, stop=(k == 7)), "pe")
        buf = scr(2) if f % 2 == 0 else scr(3)
        dve(lambda e, ff=f, b=buf: e.tensor_sub(b, Ax[:, ff, :], scr(0)))
        v2 = dve(lambda e, b=buf: e.tensor_mul(b, b, scr(1)))
        p.wait("scalar", "dve", v2)
        p.wait("scalar", "act", p.sems["act"])
        va3 = p.op("scalar", lambda e, ff=f, b=buf: e.activation(
            b, b, AF.Identity, bias=SA[:, ff:ff + 1], scale=CA1[:, ff:ff + 1]), "act")
        p.wait("vector", "pe", vpe)
        p.wait("vector", "act", va3)
        p.wait("vector", "dve", p.sems["dve"])
        vy = 0
        for hh in range(NH2):
            vy = p.op("vector", lambda e, ff=f, h=hh, b0=kb, b_=buf: e.tensor_add(
                ybf[ff][:, h * W1:h * W1 + W1],
                b_[:, h * W1:h * W1 + W1], pb[b0[h]][:, 0:W1]), "dve")
        for b in kbanks:
            bank_free[b] = ("dve", vy)
    v_y = p.sems["dve"]

    p.wait("scalar", "dve", v_y)
    ray, rby, vrow = ln_stats(ybf, sq_bf, persist=False)
    expand_rows(ray, rby, vrow)
    for f in range(NF):
        buf = scr(2) if f % 2 == 0 else scr(3)
        dve(lambda e, ff=f, b=buf: e.tensor_sub(b, ybf[ff], scr(0)))
        v2 = dve(lambda e, b=buf: e.tensor_mul(b, b, scr(1)))
        p.wait("scalar", "dve", v2)
        p.wait("scalar", "act", p.sems["act"])
        va3 = p.op("scalar", lambda e, ff=f, b=buf: e.activation(
            b, b, AF.Identity, bias=GAB[:, ff:ff + 1], scale=GAG[:, ff:ff + 1]), "act")
        p.wait("vector", "act", va3)
        dve(lambda e, ff=f, b=buf: e.tensor_add(Ax[:, ff, :], Ax[:, ff, :], b))
    v_xa = p.sems["dve"]

    x2bf = [Ay_bf[:, f * T:(f + 1) * T] for f in range(NF)]
    p.wait("vector", "dve", v_xa)
    p.wait("vector", "pe", p.sems["pe"])
    p.wait("vector", "act", p.sems["act"])
    for f in range(NF):
        p.op("vector", lambda e, ff=f: e.tensor_copy(x2bf[ff], Ax[:, ff, :]), "dve")
    v_x2bf = p.sems["dve"]
    p.wait("scalar", "dve", v_x2bf)
    ra2, rb2, vrow = ln_stats(x2bf, sq_bf, persist=False)
    expand_rows(ra2, rb2, vrow)
    h2 = [Abf[:, f, :] for f in range(NF)]
    p.wait("vector", "pe", p.sems["pe"])
    pe_claim(list(range(NT)))
    p.wait("tensor", "dp", v_params)
    vg = 0
    vgf = [0] * NF
    for f in range(NF):
        buf = scr(2) if f % 2 == 0 else scr(3)
        if f >= 2:
            p.wait("vector", "pe", vgf[f - 2])
        dve(lambda e, ff=f, b=buf: e.tensor_sub(b, Ax[:, ff, :], scr(0)))
        v2 = dve(lambda e, b=buf: e.tensor_mul(b, b, scr(1)))
        p.wait("scalar", "dve", v2)
        p.wait("scalar", "act", p.sems["act"])
        p.op("scalar", lambda e, ff=f, b=buf: e.activation(
            h2[ff], b, AF.Identity, bias=SB_[:, ff:ff + 1], scale=CB1[:, ff:ff + 1]), "act")
        vh2 = p.op("scalar", lambda e, ff=f, b=buf: e.activation(
            b, b, AF.Identity, bias=SB_[:, ff:ff + 1], scale=CB1[:, ff:ff + 1]), "act")
        p.wait("tensor", "act", vh2)
        for t in range(NT):
            vg = p.op("tensor", lambda e, ff=f, tt=t, b=buf: e.matmul(
                bank(tt)[0:E, 0:TS], wg_sb[:, ff * E:(ff + 1) * E],
                b[:, tt * TS:(tt + 1) * TS], start=(ff == 0), stop=(ff == NF - 1)), "pe")
        vgf[f] = vg
    g8 = scr(2)[0:8, :]
    p.wait("vector", "pe", p.sems["pe"])
    p.wait("vector", "act", p.sems["act"])
    p.wait("vector", "dve", p.sems["dve"])
    for t in range(NT):
        vge = p.op("vector", lambda e, tt=t: e.tensor_copy(
            g8[:, tt * TS:(tt + 1) * TS], bank(tt)[0:E, 0:TS]), "dve")
        bank_free[t] = ("dve", vge)
    tb = pb[3]
    pe_claim([6, 7])
    p.wait("tensor", "dve", max(p.sems["dve"], v_id))
    vtr = 0
    for c in range(TC):
        vtr = p.op("tensor", lambda e, cc_=c: e.transpose(
            tb[:, cc_ * E:(cc_ + 1) * E], g8[:, cc_ * 128:(cc_ + 1) * 128], ident[0:E, 0:E]), "pe")
    p.wait("vector", "pe", vtr)
    p.wait("vector", "dve", p.sems["dve"])
    p.op("vector", lambda e: e.tensor_copy(
        gtm[:].rearrange("p a b -> p (a b)"), tb[:, 0:TC * E]), "dve")
    bank_free[6] = ("dve", p.sems["dve"])
    bank_free[7] = ("dve", p.sems["dve"])

    dve(lambda e: e.reduce_max(st1[:, :, 0:1], gtm[:], axis=mybir.AxisListType.X))
    for c in range(TC):
        dve(lambda e, cc_=c: e.tensor_scalar(
            st2[:, cc_, :], gtm[:, cc_, :], st1[:, cc_, 0:1], None, ALU.subtract))
    p.wait("scalar", "dve", p.sems["dve"])
    p.wait("scalar", "act", p.sems["act"])
    vex = p.op("scalar", lambda e: e.activation(st2[:], st2[:], AF.Exp), "act")
    p.wait("vector", "act", vex)
    dve(lambda e: e.reduce_sum(st1[:, :, 0:1], st2[:], axis=mybir.AxisListType.X))
    dve(lambda e: e.reciprocal(st1[:, :, 1:2], st1[:, :, 0:1]))
    for c in range(TC):
        dve(lambda e, cc_=c: e.tensor_scalar(
            gtm[:, cc_, :], st2[:, cc_, :], st1[:, cc_, 1:2], None, ALU.mult))
    dve(lambda e: e.reduce_max(st1[:, :, 0:1], gtm[:], axis=mybir.AxisListType.X))
    for c in range(TC):
        dve(lambda e, cc_=c: e.tensor_scalar(
            st2[:, cc_, :], gtm[:, cc_, :], st1[:, cc_, 0:1], None, ALU.is_ge))
    dve(lambda e: e.scalar_tensor_tensor(st2[:], st2[:], -1e30, gtm[:], ALU.mult, ALU.add))
    dve(lambda e: e.reduce_max(st1[:, :, 2:3], st2[:], axis=mybir.AxisListType.X))
    for c in range(TC):
        dve(lambda e, cc_=c: e.tensor_scalar(
            st2[:, cc_, :], gtm[:, cc_, :], st1[:, cc_, 2:3], None, ALU.is_ge))
    dve(lambda e: e.tensor_mul(wtm[:], gtm[:], st2[:]))
    dve(lambda e: e.tensor_add(st1[:, :, 3:4], st1[:, :, 0:1], st1[:, :, 2:3]))
    dve(lambda e: e.tensor_scalar_add(st1[:, :, 3:4], st1[:, :, 3:4], 1e-8))
    dve(lambda e: e.reciprocal(st1[:, :, 4:5], st1[:, :, 3:4]))
    for c in range(TC):
        dve(lambda e, cc_=c: e.tensor_scalar(
            wtm[:, cc_, :], wtm[:, cc_, :], st1[:, cc_, 4:5], None, ALU.mult))
    v_wtm = p.sems["dve"]
    wTs = sq_bf_t[0:8, :]
    pe_claim([6, 7])
    p.wait("tensor", "dve", v_wtm)
    bs_tr = min(8, TC)
    for half in range(TC // bs_tr):
        vtr = 0
        for c8 in range(bs_tr):
            cc = half * bs_tr + c8
            vtr = p.op("tensor", lambda e, cc_=cc, j=c8: e.transpose(
                tb[0:E, j * 128:(j + 1) * 128], wtm[:, cc_, :], ident), "pe")
        p.wait("vector", "pe", vtr)
        p.wait("vector", "dve", p.sems["dve"])
        p.op("vector", lambda e, h=half, w_=bs_tr * 128: e.tensor_copy(
            wTs[:, h * w_:(h + 1) * w_], tb[0:E, 0:w_]), "dve")
        if half + 1 < TC // bs_tr:
            p.wait("tensor", "dve", p.sems["dve"])
    bank_free[6] = ("dve", p.sems["dve"])
    bank_free[7] = ("dve", p.sems["dve"])
    v_wT_dve = p.sems["dve"]

    slots = [Aw[:, i * 2048:(i + 1) * 2048] for i in range(6)]
    slot_state = [None] * 6
    slot_sems = [f"r{i}" for i in range(6)]
    slot_loads = [0] * 6
    p.wait("gpsimd", "pe", p.sems["pe"])

    def load_slab(slot, in_ap, a):
        if slot_state[slot] is not None:
            p.wait("gpsimd", *slot_state[slot])
            slot_state[slot] = None
        slot_loads[slot] += 1
        return p.dma("gpsimd", w_chunks(slots[slot], a), in_ap, slot_sems[slot])

    NB = max(1, T // 1024)
    NTH = max(1, W1 // TS)
    NU = 8 * NB
    wexp_b = Ay_bf[:, 0:T]
    t1_t = [Ay_bf[:, T + i * W1:T + (i + 1) * W1] for i in range(2)]
    s_bf = [Ay_bf[:, T + 4 * W1:T + 5 * W1] for i in range(1)]
    F_t = [Ascr_bf[:, i * T:(i + 1) * T] for i in range(8)]
    gv_slot_seq = [(2, 3), (4, 5), (0, 1), (2, 3)]
    w2_slots = [4, 5, 0, 1]
    wT_bf = sq_bf_t[0:8, :]

    def unit_ap(u, th=None):
        pt = pb[u % 4]
        if th is None:
            return pt[:, 0:W1]
        return pt[:, th * TS:th * TS + TS]

    def unit_banks(u):
        return [2 * (u % 4) + i for i in range(max(1, W1 // 512))]

    def emit_V(pend):
        ui, itl, tp, im, s12, s12_wait = pend
        pe_claim(unit_banks(ui))
        p.wait("tensor", slot_sems[s12], s12_wait)
        vv = 0
        for kc in range(8):
            for th in range(NTH):
                tt = tp * NTH + th
                vv = p.op("tensor", lambda e, k=kc, s_=s12, m=im, uu=ui, hh=th, t_=tt: e.matmul(
                    unit_ap(uu, hh), w_chunks(slots[s_], 8)[:, k, m * 128:(m + 1) * 128],
                    h2[k][:, t_ * TS:(t_ + 1) * TS],
                    start=(k == 0), stop=(k == 7)), "pe")
        if im == 1 and tp == NB - 1:
            slot_state[s12] = ("pe", vv)
        p.wait("vector", "pe", vv)
        p.wait("vector", "dve", p.sems["dve"])
        vF = p.op("vector", lambda e, uu=ui, ff=itl, t0=tp: e.tensor_mul(
            F_t[ff][:, t0 * W1:t0 * W1 + W1], t1_t[uu % 2][:, 0:W1], unit_ap(uu)), "dve")
        for b in unit_banks(ui):
            bank_free[b] = ("dve", vF)

    p.wait("tensor", "dve", v_wT_dve)
    v_wexp_used = 0
    for ei in range(E):
        wbanks = [4 + i for i in range(max(1, (2 * T) // 1024))][:4]
        pe_claim(wbanks)
        p.wait("tensor", "dve", v_wexp_used)
        p.wait("tensor", "dve", max(v_sel, v_wT_dve))
        vwe_pe = 0
        for t in range(NT):
            vwe_pe = p.op("tensor", lambda e, tt=t, ee=ei: e.matmul(
                pb[2 + tt // 2][:, (tt % 2) * TS:(tt % 2) * TS + TS] if T > 512 else pb[2][:, 0:TS],
                sel_bf[:, ee * 128:(ee + 1) * 128], wT_bf[0:8, tt * TS:(tt + 1) * TS],
                start=True, stop=True), "pe")
        p.wait("vector", "pe", vwe_pe)
        p.wait("vector", "dve", p.sems["dve"])
        vwe = 0
        for hh in range(NB):
            vwe = p.op("vector", lambda e, h=hh: e.tensor_copy(
                wexp_b[:, h * W1:h * W1 + W1],
                pb[2 + h][:, 0:W1] if T > 512 else pb[2][:, 0:W1]), "dve")
        for b in wbanks:
            bank_free[b] = ("dve", vwe)
        for ih in range(2):
            pend = None
            for it2 in range(4):
                s11, s12 = gv_slot_seq[it2]
                base_i = ih * 1024 + it2 * 256
                load_slab(s11, dW11[ei, :, base_i:base_i + 256].rearrange("(k p) c -> p k c", p=128), 8)
                load_slab(s12, dW12[ei, :, base_i:base_i + 256].rearrange("(k p) c -> p k c", p=128), 8)
                for im in range(2):
                    itl = it2 * 2 + im
                    for tp in range(NB):
                        ui = itl * NB + tp
                        pe_claim(unit_banks(ui))
                        p.wait("tensor", slot_sems[s11], 16 * slot_loads[s11])
                        vg = 0
                        for kc in range(8):
                            for th in range(NTH):
                                tt = tp * NTH + th
                                vg = p.op("tensor", lambda e, k=kc, s_=s11, m=im, uu=ui, hh=th, t_=tt: e.matmul(
                                    unit_ap(uu, hh), w_chunks(slots[s_], 8)[:, k, m * 128:(m + 1) * 128],
                                    h2[k][:, t_ * TS:(t_ + 1) * TS],
                                    start=(k == 0), stop=(k == 7)), "pe")
                        if im == 1 and tp == NB - 1:
                            slot_state[s11] = ("pe", vg)
                        p.wait("scalar", "pe", vg)
                        p.wait("scalar", "dve", p.sems["dve"])
                        p.wait("scalar", "act", p.sems["act"])
                        vs = p.op("scalar", lambda e, uu=ui: e.activation(
                            s_bf[0][:, 0:W1], unit_ap(uu), AF.Sigmoid), "act")
                        p.wait("vector", "act", vs)
                        p.wait("vector", "dve", p.sems["dve"])
                        vt1a = p.op("vector", lambda e, uu=ui: e.tensor_mul(
                            t1_t[uu % 2][:, 0:W1], unit_ap(uu), s_bf[0][:, 0:W1]), "dve")
                        p.wait("vector", "dve", vt1a)
                        vt1b = p.op("vector", lambda e, uu=ui, t0=tp: e.tensor_mul(
                            t1_t[uu % 2][:, 0:W1], t1_t[uu % 2][:, 0:W1],
                            wexp_b[:, t0 * W1:t0 * W1 + W1]), "dve")
                        for b in unit_banks(ui):
                            bank_free[b] = ("dve", vt1a)
                        if pend is not None:
                            emit_V(pend)
                        pend = (ui, itl, tp, im, s12, 16 * slot_loads[s12])
            emit_V(pend)
            pend = None
            v_F = p.sems["dve"]
            for ic2 in range(4):
                s2 = w2_slots[ic2]
                base_ic = ih * 1024 + ic2 * 256
                load_slab(s2, dW2[ei, base_ic:base_ic + 256, :].rearrange("(a p) c -> p a c", p=128), 2)
            for s2 in w2_slots:
                p.wait("tensor", slot_sems[s2], 16 * slot_loads[s2])
            p.wait("tensor", "dve", v_F)
            last_pe = 0
            for f in range(NF):
                obs = [4 * (f % 2) + t for t in range(NT)]
                pe_claim(obs)
                vpe = 0
                for ic in range(8):
                    for t in range(NT):
                        vpe = p.op("tensor", lambda e, i0=ic, ff=f, tt=t, b=obs[t]: e.matmul(
                            bank(b)[:, 0:TS],
                            w_chunks(slots[w2_slots[i0 // 2]], 2)[:, i0 % 2, ff * 128:(ff + 1) * 128],
                            F_t[i0][:, tt * TS:(tt + 1) * TS],
                            start=(i0 == 0), stop=(i0 == 7)), "pe")
                p.wait("vector", "pe", vpe)
                p.wait("vector", "dve", p.sems["dve"])
                vev = 0
                for t in range(NT):
                    vev = p.op("vector", lambda e, ff=f, tt=t, b=obs[t]: e.scalar_tensor_tensor(
                        Ax[:, ff, tt * TS:(tt + 1) * TS], bank(b)[:, 0:TS],
                        GB[:, ff:ff + 1], Ax[:, ff, tt * TS:(tt + 1) * TS],
                        ALU.mult, ALU.add), "dve")
                for b in obs:
                    bank_free[b] = ("dve", vev)
                last_pe = vpe
            for ic2 in range(4):
                slot_state[w2_slots[ic2]] = ("pe", last_pe)
        v_wexp_used = p.sems["dve"]

    v_out = p.sems["dve"]
    p.wait("sync", "dve", v_out)
    for f in range(NF):
        p.dma("sync", dy[f * 128:(f + 1) * 128, :], Ax[:, f, :], "dst")
    p.wait("sync", "dst", p.sems["dst"])

    for name in p.sems:
        p.sem_handles[name] = es.enter_context(nc.semaphore(name))
    with es:
        block = es.enter_context(nc.Block())

        @block.sync
        def _(e):
            for fn in p.thunks["sync"]:
                fn(e)

        @block.gpsimd
        def _(e):
            for fn in p.thunks["gpsimd"]:
                fn(e)

        @block.tensor
        def _(e):
            for fn in p.thunks["tensor"]:
                fn(e)

        @block.vector
        def _(e):
            for fn in p.thunks["vector"]:
                fn(e)

        @block.scalar
        def _(e):
            for fn in p.thunks["scalar"]:
                fn(e)

    return nc


_cache = {}


def kernel(L0_attn_x, L0_attn_Wq, L0_attn_Wk, L0_attn_Wv, L0_attn_Wbeta,
           L0_attn_b_beta, L0_attn_Wo, L0_attn_b_o, L0_attn_ln_g, L0_attn_ln_b,
           L0_ffn_Wg, L0_ffn_W11, L0_ffn_W12, L0_ffn_W2, L0_c, L0_wmod):
    x = np.asarray(L0_attn_x, dtype=np.float32)
    B, S, _ = x.shape
    assert B == 1 and S % NCORES == 0
    T = S // NCORES
    if "nc" not in _cache or _cache.get("T") != T:
        _cache["nc"] = build_program(T=T, TS=min(512, T))
        _cache["T"] = T
    nc = _cache["nc"]

    Wq_full = np.ascontiguousarray(L0_attn_Wq, np.float32)
    Wv_full = np.ascontiguousarray(L0_attn_Wv, np.float32)
    wmod_full = np.ascontiguousarray(L0_wmod, np.float32)
    shared = {
        "Wk": np.ascontiguousarray(L0_attn_Wk, np.float32),
        "Wbeta": np.ascontiguousarray(L0_attn_Wbeta, np.float32),
        "b_beta": np.ascontiguousarray(L0_attn_b_beta, np.float32),
        "Wo": np.ascontiguousarray(L0_attn_Wo, np.float32),
        "b_o": np.ascontiguousarray(L0_attn_b_o, np.float32),
        "ln_g": np.ascontiguousarray(L0_attn_ln_g, np.float32),
        "ln_b": np.ascontiguousarray(L0_attn_ln_b, np.float32),
        "Wg": np.ascontiguousarray(L0_ffn_Wg, np.float32),
        "W11": np.ascontiguousarray(L0_ffn_W11, np.float32),
        "W12": np.ascontiguousarray(L0_ffn_W12, np.float32),
        "W2": np.ascontiguousarray(L0_ffn_W2, np.float32),
        "cvec": np.ascontiguousarray(L0_c, np.float32),
    }
    MSH = (6 * D) // NCORES
    in_maps = []
    for cid in range(NCORES):
        xT = np.ascontiguousarray(x[0, cid * T:(cid + 1) * T, :].T)
        import ml_dtypes
        pmask = np.zeros((128, NCORES * NF), ml_dtypes.bfloat16)
        for j in range(NCORES):
            if j < cid:
                pmask[:, j * NF:(j + 1) * NF] = 1.0
        in_maps.append({**shared, "xT": xT, "pmask": pmask,
                        "Wq": np.ascontiguousarray(Wq_full[cid * 128:(cid + 1) * 128, :]),
                        "Wv": np.ascontiguousarray(Wv_full[cid * 128:(cid + 1) * 128, :]),
                        "wmod": np.ascontiguousarray(wmod_full[:, cid * MSH:(cid + 1) * MSH])})

    res = run_bass_kernel_spmd(nc, in_maps, core_ids=list(range(NCORES)))
    out = np.empty((1, S, D), np.float32)
    for cid in range(NCORES):
        out[0, cid * T:(cid + 1) * T, :] = res.results[cid]["yT"].T
    return out
